# version 03
# baseline (speedup 1.0000x reference)
import numpy as np
import ml_dtypes
from contextlib import ExitStack

import concourse.bass as bass
import concourse.mybir as mybir
from concourse.bass_utils import run_bass_kernel_spmd

F32 = mybir.dt.float32
BF16 = mybir.dt.bfloat16
I32 = mybir.dt.int32
U32 = mybir.dt.uint32
AF = mybir.ActivationFunctionType
ALU = mybir.AluOpType
AX = mybir.AxisListType

D = 4096
S = 4096
H = 32
HD = 128
ID = 11008
GRP = 64
T = 2048
NDC = D // 128
NTC = T // 512
NIB = ID // 128
HP = D // 2
IP = ID // 2
EXP_BIAS = -20.0
QK_SCALE = 1.0 / float(np.sqrt(HD))
LN_EPS = 1e-5

DEBUG_TAPS = []
PHASE_STOP = None
DEV_ATTN_GROUPS = None


def _taps_want(name):
    return name in DEBUG_TAPS


class St:
    def __init__(self, nc):
        self.nc = nc
        self.ct = {}
        self.sems = {}
        self.bar_n = 0
        self.taps = {}

    def sem(self, name):
        if name not in self.sems:
            hs = self.nc.semaphore(name).__enter__()
            self.sems[name] = hs
            self.ct[name] = 0
        return self.sems[name]

    def bump(self, sem, k=1):
        self.ct[sem.name] += k
        return self.ct[sem.name]

    def val(self, sem):
        return self.ct[sem.name]

    def dma(self, eng, out, in_, sem):
        eng.dma_start(out=out, in_=in_).then_inc(sem, 16)
        return self.bump(sem, 16)

    def barrier(self, blk):
        bar = self.sem("bar")
        self.bar_n += 1
        tgt = 5 * self.bar_n

        def mk(engname):
            def _(eng):
                eng.nop().then_inc(bar, 1)
                eng.wait_ge(bar, tgt)
            return _
        blk.sync(mk("sync"))
        blk.tensor(mk("tensor"))
        blk.scalar(mk("scalar"))
        blk.vector(mk("vector"))
        blk.gpsimd(mk("gpsimd"))
        self.ct["bar"] = tgt


def phase_init(st, ctx):
    nc = st.nc
    e = ctx.enter_context
    st.onesb = e(nc.sbuf_tensor("c_onesb", [128, 1], BF16))
    st.onesr = e(nc.sbuf_tensor("c_onesr", [1, 128], F32))
    st.ones128 = e(nc.sbuf_tensor("c_ones128", [128, 128], BF16))
    st.identb = e(nc.sbuf_tensor("c_identb", [128, 128], BF16))
    st.cvals = e(nc.sbuf_tensor("c_cvals", [128, 4], F32))
    st.EPS = st.cvals[:, 0:1]
    st.EXPB = st.cvals[:, 1:2]
    ic_ = st.sem("initc")
    with nc.Block() as blk:
        @blk.gpsimd
        def _(g):
            g.memset(st.onesb[:, :], 1.0)
            g.memset(st.onesr[:, :], 1.0)
            g.memset(st.cvals[:, 0:1], LN_EPS)
            g.memset(st.cvals[:, 1:2], EXP_BIAS)
            g.memset(st.ones128[:, :], 1.0).then_inc(ic_, 1)
            st.bump(ic_, 1)
            g.wait_ge(ic_, st.val(ic_))
            g.affine_select(st.identb[:, :], st.ones128[:, :], [[1, 128]],
                            ALU.is_equal, 0.0, channel_multiplier=-1)
    st.barrier_blk()


def phase_ln(st, ctx, src_h, xn_sb, tag):
    nc = st.nc
    ldb = [st.sem("ln_ldb0"), st.sem("ln_ldb1")]
    sc = st.sem("ln_sc")
    vc = st.sem("ln_vc")
    mm = st.sem("ln_mm")
    ldt = {}

    with ExitStack() as pctx:
        e = pctx.enter_context
        xt = [e(nc.sbuf_tensor(f"{tag}_xt{i}", [128, 512], F32)) for i in range(2)]
        xb = [e(nc.sbuf_tensor(f"{tag}_xb{i}", [128, 512], BF16)) for i in range(2)]
        sq = [e(nc.sbuf_tensor(f"{tag}_sq{i}", [128, 512], BF16)) for i in range(2)]
        stats = e(nc.sbuf_tensor(f"{tag}_stats", [1, 2048], F32))
        stats2 = e(nc.sbuf_tensor(f"{tag}_stats2", [1, 2048], F32))
        mvec = e(nc.sbuf_tensor(f"{tag}_m", [1, 2048], F32))
        rvec = e(nc.sbuf_tensor(f"{tag}_r", [1, 2048], F32))
        mr = e(nc.sbuf_tensor(f"{tag}_mr", [1, 2048], F32))
        tmp1 = e(nc.sbuf_tensor(f"{tag}_tmp", [1, 2048], F32))
        rbc = e(nc.sbuf_tensor(f"{tag}_rbc", [128, 2048], F32))
        mrbc = e(nc.sbuf_tensor(f"{tag}_mrbc", [128, 2048], F32))
        app = [e(nc.sbuf_tensor(f"{tag}_app{i}", [128, 512], F32)) for i in range(2)]
        sps = e(nc.psum_tensor(f"{tag}_sps", [1, 512], F32))
        qps = e(nc.psum_tensor(f"{tag}_qps", [1, 512], F32))
        bps = [e(nc.psum_tensor(f"{tag}_bps{i}", [128, 512], F32)) for i in range(2)]

        onesb = st.onesb
        onesr = st.onesr
        sc0 = st.val(sc)
        mm0 = st.val(mm)

        with nc.Block() as blk:
            @blk.sync
            def _(sy):
                n = 0
                for tc in range(NTC):
                    for dc in range(NDC):
                        p = n % 2
                        if n >= 2:
                            sy.wait_ge(sc, sc0 + 2 * (n - 1))
                        sy.dma_start(out=xt[p][:, :],
                                     in_=src_h[dc * 128:(dc + 1) * 128,
                                               tc * 512:(tc + 1) * 512]
                                     ).then_inc(ldb[p], 16)
                        ldt[n] = (ldb[p], st.bump(ldb[p], 16))
                        n += 1

            @blk.scalar
            def _(sa):
                n = 0
                for tc in range(NTC):
                    for dc in range(NDC):
                        p = n % 2
                        sa.wait_ge(*ldt[n])
                        if n >= 2:
                            sa.wait_ge(mm, mm0 + (n - 1))
                        sa.activation(sq[p][:, :], xt[p][:, :], AF.Square).then_inc(sc, 1)
                        sa.copy(xb[p][:, :], xt[p][:, :]).then_inc(sc, 1)
                        st.bump(sc, 2)
                        n += 1

            vc0 = st.val(vc)

            @blk.tensor
            def _(te):
                n = 0
                for tc in range(NTC):
                    if tc >= 1:
                        te.wait_ge(vc, vc0 + tc)
                    for dc in range(NDC):
                        p = n % 2
                        te.wait_ge(sc, sc0 + 2 * (n + 1))
                        te.matmul(sps[:, :], onesb[:, :], xb[p][:, :],
                                  start=(dc == 0), stop=(dc == NDC - 1))
                        te.matmul(qps[:, :], onesb[:, :], sq[p][:, :],
                                  start=(dc == 0), stop=(dc == NDC - 1)
                                  ).then_inc(mm, 1)
                        st.bump(mm, 1)
                        n += 1

            @blk.vector
            def _(ve):
                for tc in range(NTC):
                    ve.wait_ge(mm, mm0 + (tc + 1) * NDC)
                    ve.tensor_copy(stats[:, tc * 512:(tc + 1) * 512], sps[:, :])
                    ve.tensor_copy(stats2[:, tc * 512:(tc + 1) * 512], qps[:, :]).then_inc(vc, 1)
                    st.bump(vc, 1)
                ve.wait_ge(vc, st.val(vc))
                ve.tensor_scalar_mul(mvec[:, :], stats[:, :], 1.0 / D).then_inc(vc, 1)
                st.bump(vc, 1)
                ve.tensor_scalar_mul(tmp1[:, :], stats2[:, :], 1.0 / D).then_inc(vc, 1)
                st.bump(vc, 1)
                ve.wait_ge(vc, st.val(vc))
                ve.tensor_tensor(rvec[:, :], mvec[:, :], mvec[:, :], ALU.mult).then_inc(vc, 1)
                st.bump(vc, 1)
                ve.wait_ge(vc, st.val(vc))
                ve.tensor_tensor(tmp1[:, :], tmp1[:, :], rvec[:, :], ALU.subtract).then_inc(vc, 1)
                st.bump(vc, 1)

            vc_var = st.val(vc)

            @blk.scalar
            def _(sa):
                sa.wait_ge(vc, vc_var)
                sa.activation(tmp1[:, :], tmp1[:, :], AF.Sqrt, bias=st.EPS[0:1, :]).then_inc(sc, 1)
                st.bump(sc, 1)

            sc_sqrt = st.val(sc)

            @blk.vector
            def _(ve):
                ve.wait_ge(sc, sc_sqrt)
                ve.reciprocal(rvec[:, :], tmp1[:, :]).then_inc(vc, 1)
                st.bump(vc, 1)
                ve.wait_ge(vc, st.val(vc))
                ve.tensor_tensor(mr[:, :], mvec[:, :], rvec[:, :], ALU.mult).then_inc(vc, 1)
                st.bump(vc, 1)

            vc_mr = st.val(vc)
            mm_bc0 = st.val(mm)
            sc_bc0 = st.val(sc)

            @blk.tensor
            def _(te):
                te.wait_ge(vc, vc_mr)
                for tc in range(NTC):
                    if tc >= 1:
                        te.wait_ge(sc, sc_bc0 + tc)
                    te.matmul(bps[0][:, :], onesr[:, :],
                              rvec[0:1, tc * 512:(tc + 1) * 512],
                              start=True, stop=True)
                    te.matmul(bps[1][:, :], onesr[:, :],
                              mr[0:1, tc * 512:(tc + 1) * 512],
                              start=True, stop=True).then_inc(mm, 1)
                    st.bump(mm, 1)

            @blk.scalar
            def _(sa):
                for tc in range(NTC):
                    sa.wait_ge(mm, mm_bc0 + tc + 1)
                    sa.copy(rbc[:, tc * 512:(tc + 1) * 512], bps[0][:, :])
                    sa.copy(mrbc[:, tc * 512:(tc + 1) * 512], bps[1][:, :]).then_inc(sc, 1)
                    st.bump(sc, 1)

            sc1 = st.val(sc)
            vc1 = st.val(vc)

            @blk.sync
            def _(sy):
                n = 0
                for dc in range(NDC):
                    for tc in range(NTC):
                        p = n % 2
                        if n < 2:
                            sy.wait_ge(sc, sc1)
                        else:
                            sy.wait_ge(vc, vc1 + (n - 1) * 2)
                        sy.dma_start(out=xt[p][:, :],
                                     in_=src_h[dc * 128:(dc + 1) * 128,
                                               tc * 512:(tc + 1) * 512]
                                     ).then_inc(ldb[p], 16)
                        ldt[1000 + n] = (ldb[p], st.bump(ldb[p], 16))
                        n += 1

            @blk.vector
            def _(ve):
                ve.wait_ge(sc, sc1)
                n = 0
                vc2 = st.val(vc)
                for dc in range(NDC):
                    for tc in range(NTC):
                        p = n % 2
                        ve.wait_ge(*ldt[1000 + n])
                        if n >= 2:
                            ve.wait_ge(vc, vc2 + 2 * (n - 1))
                        ve.tensor_tensor(app[p][:, :], xt[p][:, :],
                                         rbc[:, tc * 512:(tc + 1) * 512], ALU.mult).then_inc(vc, 1)
                        st.bump(vc, 1)
                        ve.wait_ge(vc, st.val(vc))
                        ve.tensor_tensor(
                            xn_sb[:, dc * T + tc * 512: dc * T + (tc + 1) * 512],
                            app[p][:, :], mrbc[:, tc * 512:(tc + 1) * 512],
                            ALU.subtract).then_inc(vc, 1)
                        st.bump(vc, 1)
                        n += 1

    st.barrier_blk()


def phase_ffn_a(st, ctx, xn_sb, pass_):
    nc = st.nc
    tagp = f"fa{pass_}"
    sem_tag = "fa"
    pkb = [st.sem(sem_tag + "_pkb0"), st.sem(sem_tag + "_pkb1")]
    slb = [st.sem(sem_tag + "_slb0"), st.sem(sem_tag + "_slb1")]
    wtb = [st.sem(sem_tag + "_wtb%d" % i) for i in range(3)]
    cwb = [st.sem(sem_tag + "_cwb%d" % i) for i in range(3)]
    pkt, slt, wtt, cwt = {}, {}, {}, {}
    vq = st.sem(sem_tag + "_vq")
    sq = st.sem(sem_tag + "_sq")
    dqv = st.sem(sem_tag + "_dqv")
    tt = st.sem(sem_tag + "_tt")
    te = st.sem(sem_tag + "_te")
    mmg = st.sem(sem_tag + "_mmg")
    sil = st.sem(sem_tag + "_sil")
    hv = st.sem(sem_tag + "_hv")
    hst = st.sem(sem_tag + "_hst")

    z = {s.name: st.val(s) for s in (vq, sq, dqv, tt, te, mmg, sil, hv, hst)}

    def zz(s):
        return z[s.name]

    with ExitStack() as pctx:
        e = pctx.enter_context
        WT = [e(nc.sbuf_tensor(f"{tagp}_wt{i}", [128, D], BF16)) for i in range(3)]
        sg_sb = [e(nc.sbuf_tensor(f"{tagp}_sg{i}", [128, 1024], F32)) for i in range(2)]
        h_sb = e(nc.sbuf_tensor(f"{tagp}_h", [128, T], BF16))
        gu_ps = [e(nc.psum_tensor(f"{tagp}_gu{i}", [128, 1024], F32)) for i in range(2)]
        if pass_ == 0:
            pk = [e(nc.sbuf_tensor(f"{tagp}_pk{i}", [128, HP], I32)) for i in range(2)]
            scl = [e(nc.sbuf_tensor(f"{tagp}_scl{i}", [128, GRP], F32)) for i in range(4)]
            hi_i = e(nc.sbuf_tensor(f"{tagp}_hii", [128, 512], I32))
            lo_i = e(nc.sbuf_tensor(f"{tagp}_loi", [128, 512], I32))
            hi_f = e(nc.sbuf_tensor(f"{tagp}_hif", [128, 512], F32))
            lo_f = e(nc.sbuf_tensor(f"{tagp}_lof", [128, 512], F32))
            wnat = [e(nc.sbuf_tensor(f"{tagp}_wnat{i}", [128, D], BF16))
                    for i in range(2)]
            tp_ps = [e(nc.psum_tensor(f"{tagp}_tp{i}", [128, 128], BF16))
                     for i in range(2)]

        NSEQ = 2 * NIB
        pksrc = [st.H["pg"], st.H["pu"]]
        sclsrc = [(st.H["sg"], st.H["zg"]), (st.H["su"], st.H["zu"])]
        wtsrc = [st.H["wgT"], st.H["wuT"]]

        with nc.Block() as blk:
            @blk.sync
            def _(sy):
                if pass_ == 0:
                    for seq in range(NSEQ):
                        ib, wi = divmod(seq, 2)
                        p = seq % 2
                        if seq >= 2:
                            sy.wait_ge(vq, zz(vq) + 12 * (seq - 1))
                        sy.dma_start(out=pk[p][:, :],
                                     in_=pksrc[wi][ib * 128:(ib + 1) * 128, :]
                                     ).then_inc(pkb[p], 16)
                        pkt[seq] = (pkb[p], st.bump(pkb[p], 16))
                else:
                    for seq in range(NSEQ):
                        ib, wi = divmod(seq, 2)
                        b = seq % 3
                        if seq >= 3:
                            sprev = seq - 3
                            ibp, wip = divmod(sprev, 2)
                            sy.wait_ge(mmg, zz(mmg) + 4 * ibp + 2 * wip + 2)
                        sy.dma_start(
                            out=WT[b][:, :].rearrange("p (a c) -> p a c", a=NDC),
                            in_=wtsrc[wi][:, ib * 128:(ib + 1) * 128]
                            .rearrange("(a p) c -> p a c", p=128)
                        ).then_inc(wtb[b], 16)
                        wtt[seq] = (wtb[b], st.bump(wtb[b], 16))

            def emit_cw(g, sc):
                ibc, wic = divmod(sc, 2)
                g.wait_ge(te, zz(te) + 32 * (sc + 1))
                g.dma_start(
                    out=wtsrc[wic][:, ibc * 128:(ibc + 1) * 128]
                    .rearrange("(a p) c -> p a c", p=128),
                    in_=WT[sc % 3][:, :].rearrange("p (a c) -> p a c", a=NDC)
                ).then_inc(cwb[sc % 3], 16)
                cwt[sc] = (cwb[sc % 3], st.bump(cwb[sc % 3], 16))

            def emit_hstore(g, ib):
                g.wait_ge(hv, zz(hv) + 2 * ib + 2)
                g.dma_start(out=st.H["hbuf"][ib * 128:(ib + 1) * 128, :],
                            in_=h_sb[:, :]).then_inc(hst, 16)
                st.bump(hst, 16)

            @blk.gpsimd
            def _(g):
                if pass_ == 0:
                    for seq in range(NSEQ):
                        ib, wi = divmod(seq, 2)
                        p = seq % 2
                        if seq >= 2:
                            g.wait_ge(dqv, zz(dqv) + seq - 1)
                        g.dma_start(out=scl[2 * p][:, :],
                                    in_=sclsrc[wi][0][ib * 128:(ib + 1) * 128, :]
                                    ).then_inc(slb[p], 16)
                        st.bump(slb[p], 16)
                        g.dma_start(out=scl[2 * p + 1][:, :],
                                    in_=sclsrc[wi][1][ib * 128:(ib + 1) * 128, :]
                                    ).then_inc(slb[p], 16)
                        slt[seq] = (slb[p], st.bump(slb[p], 16))
                        if seq >= 2:
                            emit_cw(g, seq - 2)
                        if wi == 0 and seq >= 2:
                            emit_hstore(g, seq // 2 - 1)
                    emit_hstore(g, NIB - 1)
                    emit_cw(g, NSEQ - 2)
                    emit_cw(g, NSEQ - 1)
                else:
                    for ib in range(NIB):
                        emit_hstore(g, ib)
                g.wait_ge(hst, st.val(hst))
                if pass_ == 0:
                    for s_ in cwb:
                        g.wait_ge(s_, st.val(s_))

            def emit_deq(ve, seq):
                ib, wi = divmod(seq, 2)
                p = seq % 2
                ve.wait_ge(*pkt[seq])
                ve.wait_ge(*slt[seq])
                if seq >= 2:
                    ve.wait_ge(tt, zz(tt) + 32 * (seq - 1))
                sct = scl[2 * p][:, :].unsqueeze(2)
                zct = scl[2 * p + 1][:, :].unsqueeze(2)
                for q in range(4):
                    if seq + q > 0:
                        ve.wait_ge(sq, zz(sq) + 4 * seq + q)
                    ve.tensor_single_scalar(hi_i[:, :], pk[p][:, q * 512:(q + 1) * 512],
                                            4, ALU.logical_shift_right).then_inc(vq, 1)
                    st.bump(vq, 1)
                    ve.wait_ge(vq, st.val(vq))
                    ve.tensor_single_scalar(lo_i[:, :], pk[p][:, q * 512:(q + 1) * 512],
                                            15, ALU.bitwise_and).then_inc(vq, 1)
                    st.bump(vq, 1)
                    ve.wait_ge(sq, zz(sq) + 4 * seq + q + 1)
                    zb = zct[:, q * 16:(q + 1) * 16, :].broadcast_to([128, 16, 32])
                    sb = sct[:, q * 16:(q + 1) * 16, :].broadcast_to([128, 16, 32])
                    h3 = hi_f[:, :].rearrange("p (a b) -> p a b", a=16)
                    l3 = lo_f[:, :].rearrange("p (a b) -> p a b", a=16)
                    ve.tensor_tensor(h3, h3, zb, ALU.subtract)
                    ve.tensor_tensor(l3, l3, zb, ALU.subtract).then_inc(vq, 1)
                    st.bump(vq, 1)
                    ve.wait_ge(vq, st.val(vq))
                    wn2 = wnat[wi][:, q * 1024:(q + 1) * 1024].rearrange(
                        "p (a two) -> p a two", two=2)
                    wn_e = wn2[:, :, 0:1].squeeze(2).rearrange("p (a b) -> p a b", a=16)
                    wn_o = wn2[:, :, 1:2].squeeze(2).rearrange("p (a b) -> p a b", a=16)
                    ve.tensor_tensor(wn_e, h3, sb, ALU.mult)
                    i = ve.tensor_tensor(wn_o, l3, sb, ALU.mult)
                    if q == 3:
                        i.then_inc(dqv, 1)
                        st.bump(dqv, 1)

            def emit_evac(ve, seq):
                b = seq % 3
                for hc in range(NDC):
                    ve.wait_ge(tt, zz(tt) + 32 * seq + hc + 1)
                    ve.tensor_copy(WT[b][:, hc * 128:(hc + 1) * 128],
                                   tp_ps[hc % 2][:, :]).then_inc(te, 1)
                    st.bump(te, 1)

            def emit_htt(ve, ib):
                for th in range(2):
                    ve.wait_ge(mmg, zz(mmg) + 4 * ib + 2 + th + 1)
                    ve.wait_ge(sil, zz(sil) + 2 * ib + th + 1)
                    if ib >= 1:
                        ve.wait_ge(hst, zz(hst) + 16 * ib)
                    ve.tensor_tensor(h_sb[:, th * 1024:(th + 1) * 1024],
                                     sg_sb[th][:, :], gu_ps[th][:, :],
                                     ALU.mult).then_inc(hv, 1)
                    st.bump(hv, 1)

            @blk.vector
            def _(ve):
                if pass_ == 0:
                    emit_deq(ve, 0)
                    emit_deq(ve, 1)
                    emit_evac(ve, 0)
                    emit_evac(ve, 1)
                    for ib in range(NIB):
                        if ib + 1 < NIB:
                            emit_deq(ve, 2 * ib + 2)
                            emit_evac(ve, 2 * ib + 2)
                        emit_htt(ve, ib)
                        if ib + 1 < NIB:
                            emit_deq(ve, 2 * ib + 3)
                            emit_evac(ve, 2 * ib + 3)
                else:
                    for ib in range(NIB):
                        emit_htt(ve, ib)

            @blk.scalar
            def _(sa):
                def cvts(seq):
                    for q in range(4):
                        sa.wait_ge(vq, zz(vq) + 12 * seq + 3 * q + 2)
                        sa.copy(hi_f[:, :], hi_i[:, :])
                        sa.copy(lo_f[:, :], lo_i[:, :]).then_inc(sq, 1)
                        st.bump(sq, 1)

                def silu_(ib, th):
                    sa.wait_ge(mmg, zz(mmg) + 4 * ib + th + 1)
                    if ib >= 1:
                        sa.wait_ge(hv, zz(hv) + 2 * (ib - 1) + th + 1)
                    sa.activation(sg_sb[th][:, :], gu_ps[th][:, :],
                                  AF.Silu).then_inc(sil, 1)
                    st.bump(sil, 1)

                if pass_ == 0:
                    cvts(0)
                    cvts(1)
                    for ib in range(NIB):
                        if ib + 1 < NIB:
                            cvts(2 * ib + 2)
                        silu_(ib, 0)
                        silu_(ib, 1)
                        if ib + 1 < NIB:
                            cvts(2 * ib + 3)
                else:
                    for ib in range(NIB):
                        silu_(ib, 0)
                        silu_(ib, 1)

            def emit_T(te_e, seq):
                ib, wi = divmod(seq, 2)
                te_e.wait_ge(dqv, zz(dqv) + seq + 1)
                if seq >= 3:
                    sprev = seq - 3
                    ibp, wip = divmod(sprev, 2)
                    te_e.wait_ge(mmg, zz(mmg) + 4 * ibp + 2 * wip + 2)
                    te_e.wait_ge(*cwt[sprev])
                for hc in range(NDC):
                    if 32 * seq + hc >= 2:
                        te_e.wait_ge(te, zz(te) + 32 * seq + hc - 1)
                    te_e.transpose(tp_ps[hc % 2][:, :],
                                   wnat[wi][:, hc * 128:(hc + 1) * 128],
                                   st.identb[:, :]).then_inc(tt, 1)
                    st.bump(tt, 1)

            def emit_mm(te_e, seq):
                ib, wi = divmod(seq, 2)
                b = seq % 3
                if pass_ == 0:
                    wt_ready = (te, zz(te) + 32 * (seq + 1))
                else:
                    wt_ready = wtt[seq]
                for th in range(2):
                    te_e.wait_ge(*wt_ready)
                    if wi == 0:
                        if ib >= 1:
                            te_e.wait_ge(hv, zz(hv) + 2 * (ib - 1) + th + 1)
                    else:
                        te_e.wait_ge(sil, zz(sil) + 2 * ib + th + 1)
                    for tc2 in range(2):
                        col = th * 1024 + tc2 * 512
                        for hc in range(NDC):
                            i = te_e.matmul(
                                gu_ps[th][:, tc2 * 512:(tc2 + 1) * 512],
                                WT[b][:, hc * 128:(hc + 1) * 128],
                                xn_sb[:, hc * T + col: hc * T + col + 512],
                                start=(hc == 0), stop=(hc == NDC - 1))
                    i.then_inc(mmg, 1)
                    st.bump(mmg, 1)

            @blk.tensor
            def _(te_e):
                if pass_ == 0:
                    emit_T(te_e, 0)
                    emit_T(te_e, 1)
                    for ib in range(NIB):
                        emit_mm(te_e, 2 * ib)
                        if ib + 1 < NIB:
                            emit_T(te_e, 2 * ib + 2)
                        emit_mm(te_e, 2 * ib + 1)
                        if ib + 1 < NIB:
                            emit_T(te_e, 2 * ib + 3)
                else:
                    for ib in range(NIB):
                        emit_mm(te_e, 2 * ib)
                        emit_mm(te_e, 2 * ib + 1)

        if pass_ == 0:
            assert st.val(vq) == zz(vq) + 12 * NSEQ
            assert st.val(sq) == zz(sq) + 4 * NSEQ
            assert st.val(dqv) == zz(dqv) + NSEQ
            assert st.val(tt) == zz(tt) + 32 * NSEQ
            assert st.val(te) == zz(te) + 32 * NSEQ
        assert st.val(mmg) == zz(mmg) + 4 * NIB
        assert st.val(sil) == zz(sil) + 2 * NIB
        assert st.val(hv) == zz(hv) + 2 * NIB
        assert st.val(hst) == zz(hst) + 16 * NIB

    st.barrier_blk()


def phase_wd(st, ctx):
    nc = st.nc
    ldb = [st.sem("wd_ldb0"), st.sem("wd_ldb1")]
    slb = [st.sem("wd_slb0"), st.sem("wd_slb1")]
    ldt, slt = {}, {}
    vq = st.sem("wd_vq")
    sq = st.sem("wd_sq")
    dqv = st.sem("wd_dqv")
    tt = st.sem("wd_tt")
    te = st.sem("wd_te")
    cw = st.sem("wd_cw")
    QW = IP // 4
    NCH = 8
    chunks = [(11 * i, min(11, NIB - 11 * i)) for i in range(NCH)]

    z = {s.name: st.val(s) for s in (vq, sq, dqv, tt, te, cw)}

    def zz(s):
        return z[s.name]

    with ExitStack() as pctx:
        e = pctx.enter_context
        pq = [e(nc.sbuf_tensor(f"wd_pq{i}", [128, QW], I32)) for i in range(2)]
        scl = [e(nc.sbuf_tensor(f"wd_scl{i}", [128, IP // 32], F32)) for i in range(4)]
        hi_i = e(nc.sbuf_tensor("wd_hii", [128, QW], I32))
        lo_i = e(nc.sbuf_tensor("wd_loi", [128, QW], I32))
        hi_f = e(nc.sbuf_tensor("wd_hif", [128, QW], F32))
        lo_f = e(nc.sbuf_tensor("wd_lof", [128, QW], F32))
        wnat = e(nc.sbuf_tensor("wd_wnat", [128, ID], BF16))
        wpart = e(nc.sbuf_tensor("wd_part", [128, 11 * 128], BF16))
        tp_ps = [e(nc.psum_tensor(f"wd_tp{i}", [128, 128], BF16)) for i in range(2)]

        with nc.Block() as blk:
            @blk.sync
            def _(sy):
                for db in range(NDC):
                    for q in range(4):
                        qs = db * 4 + q
                        p = qs % 2
                        if qs >= 2:
                            sy.wait_ge(vq, zz(vq) + 3 * (qs - 1))
                        sy.dma_start(out=pq[p][:, :],
                                     in_=st.H["pd"][db * 128:(db + 1) * 128,
                                                    q * QW:(q + 1) * QW]
                                     ).then_inc(ldb[p], 16)
                        ldt[qs] = (ldb[p], st.bump(ldb[p], 16))

            @blk.gpsimd
            def _(g):
                for db in range(NDC):
                    p = db % 2
                    if db >= 2:
                        g.wait_ge(dqv, zz(dqv) + 4 * (db - 1))
                    g.dma_start(out=scl[2 * p][:, :],
                                in_=st.H["sd"][db * 128:(db + 1) * 128, :]
                                ).then_inc(slb[p], 16)
                    st.bump(slb[p], 16)
                    g.dma_start(out=scl[2 * p + 1][:, :],
                                in_=st.H["zd"][db * 128:(db + 1) * 128, :]
                                ).then_inc(slb[p], 16)
                    slt[db] = (slb[p], st.bump(slb[p], 16))
                    for ci, (c0, cnt) in enumerate(chunks):
                        g.wait_ge(te, zz(te) + NIB * db + c0 + cnt)
                        g.dma_start(
                            out=st.H["wdT"][c0 * 128:(c0 + cnt) * 128,
                                            db * 128:(db + 1) * 128]
                            .rearrange("(a p) c -> p a c", p=128),
                            in_=wpart[:, :cnt * 128].rearrange(
                                "p (a c) -> p a c", a=cnt)
                        ).then_inc(cw, 16)
                        st.bump(cw, 16)
                g.wait_ge(cw, st.val(cw))

            @blk.vector
            def _(ve):
                for db in range(NDC):
                    p = db % 2
                    sct = scl[2 * p][:, :].unsqueeze(2)
                    zct = scl[2 * p + 1][:, :].unsqueeze(2)
                    for q in range(4):
                        qs = db * 4 + q
                        ve.wait_ge(*ldt[qs])
                        ve.wait_ge(*slt[db])
                        if q == 0 and db >= 1:
                            ve.wait_ge(tt, zz(tt) + NIB * db)
                        if qs > 0:
                            ve.wait_ge(sq, zz(sq) + qs)
                        ve.tensor_single_scalar(hi_i[:, :], pq[qs % 2][:, :],
                                                4, ALU.logical_shift_right).then_inc(vq, 1)
                        st.bump(vq, 1)
                        ve.wait_ge(vq, st.val(vq))
                        ve.tensor_single_scalar(lo_i[:, :], pq[qs % 2][:, :],
                                                15, ALU.bitwise_and).then_inc(vq, 1)
                        st.bump(vq, 1)
                        ve.wait_ge(sq, zz(sq) + qs + 1)
                        ng = QW // 32
                        zb = zct[:, q * ng:(q + 1) * ng, :].broadcast_to([128, ng, 32])
                        sb = sct[:, q * ng:(q + 1) * ng, :].broadcast_to([128, ng, 32])
                        h3 = hi_f[:, :].rearrange("p (a b) -> p a b", a=ng)
                        l3 = lo_f[:, :].rearrange("p (a b) -> p a b", a=ng)
                        ve.tensor_tensor(h3, h3, zb, ALU.subtract)
                        ve.tensor_tensor(l3, l3, zb, ALU.subtract).then_inc(vq, 1)
                        st.bump(vq, 1)
                        ve.wait_ge(vq, st.val(vq))
                        wn2 = wnat[:, q * 2 * QW:(q + 1) * 2 * QW].rearrange(
                            "p (a two) -> p a two", two=2)
                        wn_e = wn2[:, :, 0:1].squeeze(2).rearrange("p (a b) -> p a b", a=ng)
                        wn_o = wn2[:, :, 1:2].squeeze(2).rearrange("p (a b) -> p a b", a=ng)
                        ve.tensor_tensor(wn_e, h3, sb, ALU.mult)
                        ve.tensor_tensor(wn_o, l3, sb, ALU.mult).then_inc(dqv, 1)
                        st.bump(dqv, 1)
                    for ic in range(NIB):
                        ve.wait_ge(tt, zz(tt) + NIB * db + ic + 1)
                        ci = min(ic // 11, NCH - 1)
                        c0 = chunks[ci][0]
                        if ic % 11 == 0 and (db > 0 or ic > 0):
                            nprev = NCH * db + ci
                            ve.wait_ge(cw, zz(cw) + 16 * nprev)
                        ve.tensor_copy(wpart[:, (ic - c0) * 128:(ic - c0 + 1) * 128],
                                       tp_ps[ic % 2][:, :]).then_inc(te, 1)
                        st.bump(te, 1)

            @blk.scalar
            def _(sa):
                for db in range(NDC):
                    for q in range(4):
                        qs = db * 4 + q
                        sa.wait_ge(vq, zz(vq) + 3 * qs + 2)
                        sa.copy(hi_f[:, :], hi_i[:, :])
                        sa.copy(lo_f[:, :], lo_i[:, :]).then_inc(sq, 1)
                        st.bump(sq, 1)

            @blk.tensor
            def _(te_e):
                for db in range(NDC):
                    te_e.wait_ge(dqv, zz(dqv) + 4 * (db + 1))
                    for ic in range(NIB):
                        if ic >= 2:
                            te_e.wait_ge(te, zz(te) + NIB * db + ic - 1)
                        te_e.transpose(tp_ps[ic % 2][:, :],
                                       wnat[:, ic * 128:(ic + 1) * 128],
                                       st.identb[:, :]).then_inc(tt, 1)
                        st.bump(tt, 1)

        assert st.val(vq) == zz(vq) + 12 * NDC
        assert st.val(dqv) == zz(dqv) + 4 * NDC
        assert st.val(tt) == zz(tt) + NIB * NDC
        assert st.val(te) == zz(te) + NIB * NDC
        assert st.val(cw) == zz(cw) + 16 * NCH * NDC

    st.barrier_blk()


def phase_ffn_b(st, ctx, pass_, src_h, dst_h):
    nc = st.nc
    tagp = f"fb{pass_}"
    hl = st.sem("fb_hl")
    wlb = [st.sem("fb_wlb0"), st.sem("fb_wlb1")]
    wlt = {}
    xl = st.sem("fb_xl")
    mmh = st.sem("fb_mmh")
    ev = st.sem("fb_ev")
    xs = st.sem("fb_xs")

    z = {s.name: st.val(s) for s in (hl, xl, mmh, ev, xs)}

    def zz(s):
        return z[s.name]

    NGEN = 2 * NDC

    with ExitStack() as pctx:
        e = pctx.enter_context
        hres = e(nc.sbuf_tensor(f"{tagp}_hres", [128, NIB * 1024], BF16))
        wdt = [e(nc.sbuf_tensor(f"{tagp}_wdt{i}", [128, 43 * 128], BF16))
               for i in range(2)]
        xres = [e(nc.sbuf_tensor(f"{tagp}_x0", [128, 1024], F32))]
        od = [e(nc.sbuf_tensor(f"{tagp}_o0", [128, 1024], F32))]
        yps = [e(nc.psum_tensor(f"{tagp}_y{i}", [128, 1024], F32)) for i in range(2)]

        with nc.Block() as blk:
            @blk.gpsimd
            def _(g):
                for th in range(2):
                    if th == 1:
                        g.wait_ge(mmh, zz(mmh) + 2 * NDC)
                    for ic in range(NIB):
                        g.dma_start(out=hres[:, ic * 1024:(ic + 1) * 1024],
                                    in_=st.H["hbuf"][ic * 128:(ic + 1) * 128,
                                                     th * 1024:(th + 1) * 1024]
                                    ).then_inc(hl, 16)
                        st.bump(hl, 16)
                    for db in range(NDC):
                        gen = th * NDC + db
                        if gen >= 1:
                            g.wait_ge(ev, zz(ev) + gen)
                        g.dma_start(out=xres[0][:, :],
                                    in_=src_h[db * 128:(db + 1) * 128,
                                              th * 1024:(th + 1) * 1024]
                                    ).then_inc(xl, 16)
                        st.bump(xl, 16)
                        g.wait_ge(ev, zz(ev) + gen + 1)
                        g.dma_start(out=dst_h[db * 128:(db + 1) * 128,
                                              th * 1024:(th + 1) * 1024],
                                    in_=od[0][:, :]).then_inc(xs, 16)
                        st.bump(xs, 16)
                g.wait_ge(xs, st.val(xs))

            @blk.sync
            def _(sy):
                subs = [(0, 11), (11, 11), (22, 11), (33, 10)]
                for th in range(2):
                    for db in range(NDC):
                        gen = th * NDC + db
                        for hf in range(2):
                            if gen > 0 or hf > 0:
                                sy.wait_ge(mmh, zz(mmh) + 2 * gen + hf - 1)
                            for (s0, scnt) in subs:
                                r0 = (hf * 43 + s0) * 128
                                sy.dma_start(
                                    out=wdt[hf][:, s0 * 128:(s0 + scnt) * 128]
                                    .rearrange("p (a c) -> p a c", a=scnt),
                                    in_=st.H["wdT"][r0:r0 + scnt * 128,
                                                    db * 128:(db + 1) * 128]
                                    .rearrange("(a p) c -> p a c", p=128)
                                ).then_inc(wlb[hf], 16)
                                wlt[(gen, hf)] = (wlb[hf], st.bump(wlb[hf], 16))

            @blk.tensor
            def _(te_e):
                for th in range(2):
                    te_e.wait_ge(hl, zz(hl) + 16 * NIB * (th + 1))
                    for db in range(NDC):
                        gen = th * NDC + db
                        p = gen % 2
                        if gen >= 2:
                            te_e.wait_ge(ev, zz(ev) + gen - 1)
                        for hf in range(2):
                            te_e.wait_ge(*wlt[(gen, hf)])
                            for j in range(43):
                                ic = hf * 43 + j
                                i = None
                                for c2 in range(2):
                                    i = te_e.matmul(
                                        yps[p][:, c2 * 512:(c2 + 1) * 512],
                                        wdt[hf][:, j * 128:(j + 1) * 128],
                                        hres[:, ic * 1024 + c2 * 512:
                                             ic * 1024 + c2 * 512 + 512],
                                        start=(ic == 0), stop=(ic == NIB - 1))
                            i.then_inc(mmh, 1)
                            st.bump(mmh, 1)

            @blk.vector
            def _(ve):
                sc_y = 0.5 if pass_ in (0, 1) else 1.0
                for th in range(2):
                    for db in range(NDC):
                        gen = th * NDC + db
                        p = gen % 2
                        ve.wait_ge(mmh, zz(mmh) + 2 * gen + 2)
                        ve.wait_ge(xl, zz(xl) + 16 * (gen + 1))
                        if gen >= 1:
                            ve.wait_ge(xs, zz(xs) + 16 * gen)
                        for c2 in range(2):
                            i = ve.scalar_tensor_tensor(
                                od[0][:, c2 * 512:(c2 + 1) * 512],
                                yps[p][:, c2 * 512:(c2 + 1) * 512],
                                sc_y, xres[0][:, c2 * 512:(c2 + 1) * 512],
                                ALU.mult, ALU.add)
                        i.then_inc(ev, 1)
                        st.bump(ev, 1)

        assert st.val(mmh) == zz(mmh) + 2 * NGEN
        assert st.val(ev) == zz(ev) + NGEN
        assert st.val(xs) == zz(xs) + 16 * NGEN

    st.barrier_blk()


def phase_qkv(st, ctx, xn_sb):
    nc = st.nc
    qlb = [st.sem("qk_qlb0"), st.sem("qk_qlb1")]
    tbl = st.sem("qk_tbl")
    qm = st.sem("qk_qm")
    qe = st.sem("qk_qe")
    qswb = [st.sem("qk_swb0"), st.sem("qk_swb1")]
    qr = st.sem("qk_qr")
    qstb = [st.sem("qk_qst0"), st.sem("qk_qst1")]
    kstb = [st.sem("qk_kst0"), st.sem("qk_kst1")]
    vstb = [st.sem("qk_vst0"), st.sem("qk_vst1")]
    pl = st.sem("qk_pl")
    qlt, qswt = {}, {}
    z = {s.name: st.val(s) for s in (qm, qe, qr)}
    z.update({s.name: st.val(s) for s in qstb + kstb + vstb})

    def zz(s):
        return z[s.name]

    wsrc = [st.H["wq"], st.H["wk"]]

    with ExitStack() as pctx:
        e = pctx.enter_context
        nonce_sb = e(nc.sbuf_tensor("qk_nonce", [1, 1], U32))
        flag_sb = e(nc.sbuf_tensor("qk_flag", [1, 1], U32))

        with ExitStack() as qctx:
            e2 = qctx.enter_context
            c128 = e2(nc.sbuf_tensor("qk_c128", [128, T], F32))
            s128 = e2(nc.sbuf_tensor("qk_s128", [128, T], F32))
            wqt = [e2(nc.sbuf_tensor(f"qk_wqt{i}", [128, D], BF16)) for i in range(2)]
            X = [e2(nc.sbuf_tensor(f"qk_X{i}", [128, T], F32)) for i in range(2)]
            Xs = [e2(nc.sbuf_tensor(f"qk_Xs{i}", [128, T], F32)) for i in range(2)]
            ro = [e2(nc.sbuf_tensor(f"qk_ro{i}", [128, T], BF16)) for i in range(2)]
            qp = [e2(nc.psum_tensor(f"qk_qp{i}", [128, T], F32)) for i in range(2)]

            NH2 = 2 * H

            with nc.Block() as blk:
                @blk.gpsimd
                def _(g):
                    pid = g.partition_id()
                    myslot = nc.s_assert_within(pid % 2, min_val=0, max_val=1)
                    g.dma_start(out=c128[:, :], in_=st.H["c128"].ap()).then_inc(tbl, 16)
                    st.bump(tbl, 16)
                    g.dma_start(out=s128[:, :], in_=st.H["s128"].ap()).then_inc(tbl, 16)
                    st.bump(tbl, 16)
                    g.dma_start(out=nonce_sb[:, :], in_=st.H["nonce"].ap()).then_inc(tbl, 16)
                    st.bump(tbl, 16)
                    for hs in range(NH2):
                        wsel, hb = divmod(hs, H)
                        p = hs % 2
                        if hs >= 2:
                            g.wait_ge(qm, zz(qm) + hs - 1)
                        g.dma_start(
                            out=wqt[p][:, :].rearrange("p (a c) -> p a c", a=NDC),
                            in_=wsrc[wsel][:, hb * 128:(hb + 1) * 128]
                            .rearrange("(a p) c -> p a c", p=128)
                        ).then_inc(qlb[p], 16)
                        qlt[hs] = (qlb[p], st.bump(qlb[p], 16))
                        g.wait_ge(qe, zz(qe) + hs + 1)
                        if hs >= 2:
                            g.wait_ge(qr, zz(qr) + 2 * (hs - 1))
                        g.dma_start(out=Xs[p][0:64, :], in_=X[p][64:128, :]
                                    ).then_inc(qswb[p], 16)
                        st.bump(qswb[p], 16)
                        g.dma_start(out=Xs[p][64:128, :], in_=X[p][0:64, :]
                                    ).then_inc(qswb[p], 16)
                        qswt[hs] = (qswb[p], st.bump(qswb[p], 16))
                        if wsel == 1:
                            g.wait_ge(qr, zz(qr) + 2 * (hs + 1))
                            g.dma_start(
                                out=st.H["k_loc"][hb * 128:(hb + 1) * 128, :],
                                in_=ro[p][:, :]).then_inc(kstb[p], 16)
                            st.bump(kstb[p], 16)

                @blk.tensor
                def _(te_e):
                    for hs in range(NH2):
                        p = hs % 2
                        te_e.wait_ge(*qlt[hs])
                        if hs >= 2:
                            te_e.wait_ge(qe, zz(qe) + hs - 1)
                        for tc in range(NTC):
                            i = None
                            for hc in range(NDC):
                                i = te_e.matmul(
                                    qp[p][:, tc * 512:(tc + 1) * 512],
                                    wqt[p][:, hc * 128:(hc + 1) * 128],
                                    xn_sb[:, hc * T + tc * 512: hc * T + (tc + 1) * 512],
                                    start=(hc == 0), stop=(hc == NDC - 1))
                        i.then_inc(qm, 1)
                        st.bump(qm, 1)

                @blk.scalar
                def _(sa):
                    for hs in range(NH2):
                        p = hs % 2
                        sa.wait_ge(qm, zz(qm) + hs + 1)
                        if hs >= 2:
                            sa.wait_ge(qr, zz(qr) + 2 * (hs - 1))
                            sa.wait_ge(*qswt[hs - 2])
                        sa.copy(X[p][:, :], qp[p][:, :]).then_inc(qe, 1)
                        st.bump(qe, 1)

                tblv = st.val(tbl)

                @blk.vector
                def _(ve):
                    ve.wait_ge(tbl, tblv)
                    for hs in range(NH2):
                        p = hs % 2
                        ve.wait_ge(*qswt[hs])
                        if hs >= 2:
                            hp = hs - 2
                            if hp < H:
                                ve.wait_ge(qstb[p], zz(qstb[p]) + 16 * (hp // 2 + 1))
                            else:
                                ve.wait_ge(kstb[p], zz(kstb[p]) + 16 * ((hp - H) // 2 + 1))
                        ve.tensor_tensor(X[p][:, :], X[p][:, :], c128[:, :],
                                         ALU.mult)
                        ve.tensor_tensor(Xs[p][:, :], Xs[p][:, :], s128[:, :],
                                         ALU.mult).then_inc(qr, 1)
                        st.bump(qr, 1)
                        ve.wait_ge(qr, st.val(qr))
                        ve.tensor_tensor(ro[p][:, :], X[p][:, :], Xs[p][:, :],
                                         ALU.add).then_inc(qr, 1)
                        st.bump(qr, 1)

                @blk.sync
                def _(sy):
                    for hs in range(H):
                        hb = hs
                        p = hs % 2
                        sy.wait_ge(qr, zz(qr) + 2 * (hs + 1))
                        sy.dma_start(out=st.H["q"][hb * 128:(hb + 1) * 128, :],
                                     in_=ro[p][:, :]).then_inc(qstb[p], 16)
                        st.bump(qstb[p], 16)

            with nc.Block() as blkz:
                @blkz.sync
                def _(sy):
                    for s_ in qstb:
                        sy.wait_ge(s_, st.val(s_))

                @blkz.gpsimd
                def _(g):
                    for s_ in kstb + qswb + qlb + [tbl]:
                        g.wait_ge(s_, st.val(s_))
            st.barrier_blk()

            assert st.val(qm) == zz(qm) + NH2
            assert st.val(qe) == zz(qe) + NH2
            assert st.val(qr) == zz(qr) + 2 * NH2
            assert st.val(qstb[0]) + st.val(qstb[1]) == zz(qstb[0]) + zz(qstb[1]) + 16 * H
            assert st.val(kstb[0]) + st.val(kstb[1]) == zz(kstb[0]) + zz(kstb[1]) + 16 * H

        z2 = {s.name: st.val(s) for s in (qm, qe)}
        z2.update({s.name: st.val(s) for s in vstb})

        def zz2(s):
            return z2[s.name]

        with ExitStack() as vctx:
            e2 = vctx.enter_context
            wvt = [e2(nc.sbuf_tensor(f"qk_wvt{i}", [128, NDC * 512], BF16))
                   for i in range(2)]
            vsb = [e2(nc.sbuf_tensor(f"qk_vsb{i}", [128, 512], BF16)) for i in range(2)]
            vp = [e2(nc.psum_tensor(f"qk_vp{i}", [128, 512], F32)) for i in range(2)]
            vlt = {}
            NTB = 16

            with nc.Block() as blk:
                @blk.gpsimd
                def _(g):
                    pid = g.partition_id()
                    myslot = nc.s_assert_within(pid % 2, min_val=0, max_val=1)
                    n = 0
                    for dv in range(8):
                        p = dv % 2
                        if dv >= 2:
                            g.wait_ge(qm, zz2(qm) + NTB * (dv - 1))
                        g.dma_start(
                            out=wvt[p][:, :].rearrange("p (a c) -> p a c", a=NDC),
                            in_=st.H["wv"][:, dv * 512:(dv + 1) * 512]
                            .rearrange("(a p) c -> p a c", p=128)
                        ).then_inc(qlb[p], 16)
                        vlt[dv] = (qlb[p], st.bump(qlb[p], 16))
                        for tb in range(NTB):
                            pb = n % 2
                            g.wait_ge(qe, zz2(qe) + n + 1)
                            dst = st.H["v_loc"][tb * 128:(tb + 1) * 128,
                                                dv * 512:(dv + 1) * 512]
                            g.dma_start(out=dst, in_=vsb[pb][:, :]).then_inc(vstb[pb], 16)
                            st.bump(vstb[pb], 16)
                            n += 1

                @blk.tensor
                def _(te_e):
                    n = 0
                    for dv in range(8):
                        p = dv % 2
                        te_e.wait_ge(*vlt[dv])
                        for tb in range(NTB):
                            pb = n % 2
                            if n >= 2:
                                te_e.wait_ge(qe, zz2(qe) + n - 1)
                            i = None
                            for hc in range(NDC):
                                i = te_e.matmul(
                                    vp[pb][:, :],
                                    xn_sb[:, hc * T + tb * 128: hc * T + (tb + 1) * 128],
                                    wvt[p][:, hc * 512:(hc + 1) * 512],
                                    start=(hc == 0), stop=(hc == NDC - 1))
                            i.then_inc(qm, 1)
                            st.bump(qm, 1)
                            n += 1

                @blk.scalar
                def _(sa):
                    n = 0
                    for dv in range(8):
                        for tb in range(NTB):
                            pb = n % 2
                            sa.wait_ge(qm, zz2(qm) + n + 1)
                            if n >= 2:
                                sa.wait_ge(vstb[pb], zz2(vstb[pb]) + 16 * ((n - 2) // 2 + 1))
                            sa.copy(vsb[pb][:, :], vp[pb][:, :]).then_inc(qe, 1)
                            st.bump(qe, 1)
                            n += 1

            assert st.val(qm) == zz2(qm) + 128
            assert st.val(qe) == zz2(qe) + 128
            assert st.val(vstb[0]) + st.val(vstb[1]) == zz2(vstb[0]) + zz2(vstb[1]) + 16 * 128

        with nc.Block() as blk:
            @blk.gpsimd
            def _(g):
                pid = g.partition_id()
                myslot = nc.s_assert_within(pid % 2, min_val=0, max_val=1)
                pslot = nc.s_assert_within(1 - myslot, min_val=0, max_val=1)
                for s_ in qstb + kstb + vstb + qlb:
                    g.wait_ge(s_, st.val(s_))
                g.dma_start(out=st.H["k_sh"][bass.ds(myslot, 1), :, :].squeeze(0),
                            in_=st.H["k_loc"].ap()).then_inc(pl, 16)
                g.dma_start(out=st.H["v_sh"][bass.ds(myslot, 1), :, :].squeeze(0),
                            in_=st.H["v_loc"].ap()).then_inc(pl, 16)
                g.wait_ge(pl, 32)
                g.dma_start(out=st.H["flg"][bass.ds(myslot, 1), :].squeeze(0),
                            in_=nonce_sb[:1, :1]).then_inc(pl, 16)
                g.wait_ge(pl, 48)
                with (g.register("fl") as fl, g.register("nn") as nn,
                      g.register("ccr") as ccr, g.register("wt") as wt):
                    g.reg_load(nn, nonce_sb[:1, :1])
                    g.reg_mov(fl, 0)
                    g.reg_mov(wt, 48)
                    g.reg_alu(ccr, fl, nn, ALU.not_equal)
                    with g.While(ccr):
                        g.dma_start(out=flag_sb[:, :],
                                    in_=st.H["flg"][bass.ds(pslot, 1), :].unsqueeze(0)
                                    ).then_inc(pl, 16)
                        g.reg_add(wt, wt, 16)
                        g.wait_ge(pl, wt)
                        g.reg_load(fl, flag_sb[:1, :1])
                        g.reg_alu(ccr, fl, nn, ALU.not_equal)

    st.barrier_blk()


def phase_attn(st, ctx, ao_sb):
    nc = st.nc
    akl = [st.sem("at_akl0"), st.sem("at_akl1")]
    avl = [st.sem("at_avl0"), st.sem("at_avl1")]
    aml = st.sem("at_aml")
    asm = st.sem("at_asm")
    aex = st.sem("at_aex")
    amk = st.sem("at_amk")
    aog = st.sem("at_aog")
    arc = st.sem("at_arc")
    aev = st.sem("at_aev")

    z = {s.name: st.val(s) for s in (asm, aex, amk, aog, arc, aev)}

    def zz(s):
        return z[s.name]

    groups = []
    for h in range(H):
        for G in range(4):
            L = 4 * G + 4
            strips = []
            for r in range(2):
                for l in range(L):
                    smin2 = max(l - 4 * G, 0)
                    masked = l >= 4 * G
                    strips.append((r, l, smin2, masked))
            groups.append((h, G, strips))
    if DEV_ATTN_GROUPS is not None:
        groups = groups[:DEV_ATTN_GROUPS]
    cumstrip = []
    cummask = []
    cs = cm = 0
    for (_h, _G, strips) in groups:
        cumstrip.append(cs)
        cummask.append(cm)
        cs += len(strips)
        cm += sum(1 for s_ in strips if s_[3])
    NSTR = cs
    NM = len(groups)

    with ExitStack() as pctx:
        e = pctx.enter_context
        q_sb = [e(nc.sbuf_tensor(f"at_q{i}", [128, 512], BF16)) for i in range(2)]
        k_sb = [e(nc.sbuf_tensor(f"at_k{i}", [128, 4096], BF16)) for i in range(2)]
        v_sb = [e(nc.sbuf_tensor(f"at_v{i}", [128, 4096], BF16)) for i in range(2)]
        pts = [e(nc.sbuf_tensor(f"at_pts{i}", [128, 512], BF16)) for i in range(2)]
        msk = e(nc.sbuf_tensor("at_msk", [128, 256], BF16))
        rinv = [e(nc.sbuf_tensor(f"at_rinv{i}", [128, 512], F32)) for i in range(2)]
        if "atdbg" in st.taps:
            st.dbg_sb = e(nc.sbuf_tensor("at_dbg", [128, 2560], F32))
        stp = [e(nc.psum_tensor(f"at_stp{i}", [128, 512], F32)) for i in range(2)]
        sums = [e(nc.psum_tensor(f"at_sums{i}", [128, 512], F32)) for i in range(2)]
        pvp = [e(nc.psum_tensor(f"at_pvp{i}", [128, 512], F32)) for i in range(2)]
        qkt, vlt = {}, {}

        with nc.Block() as blk:
            @blk.sync
            def _(sy):
                for m, (h, G, strips) in enumerate(groups):
                    p = m % 2
                    L = 4 * G + 4
                    if m >= 2:
                        sy.wait_ge(aog, zz(aog) + cumstrip[m - 1])
                    sy.dma_start(out=q_sb[p][:, :],
                                 in_=st.H["q"][h * 128:(h + 1) * 128,
                                               G * 512:(G + 1) * 512]
                                 ).then_inc(akl[p], 16)
                    st.bump(akl[p], 16)
                    for r in range(2):
                        sy.dma_start(
                            out=k_sb[p][:, r * L * 128:(r + 1) * L * 128],
                            in_=st.H["k_sh"][r, h * 128:(h + 1) * 128, 0:L * 128]
                        ).then_inc(akl[p], 16)
                        st.bump(akl[p], 16)
                    qkt[m] = (akl[p], st.val(akl[p]))

            @blk.gpsimd
            def _(g):
                g.dma_start(out=msk[:, :].rearrange("p (a c) -> p a c", a=2),
                            in_=st.H["masks"].ap().rearrange("a p c -> p a c")
                            ).then_inc(aml, 16)
                st.bump(aml, 16)
                nh = max(h_ for (h_, _g, _s) in groups) + 1
                for h in range(nh):
                    p = h % 2
                    if h >= 2:
                        g.wait_ge(aog, zz(aog) + cumstrip[4 * (h - 1) - 1]
                                  + len(groups[4 * (h - 1) - 1][2]))
                    for r in range(2):
                        g.dma_start(
                            out=v_sb[p][:, r * 2048:(r + 1) * 2048]
                            .rearrange("p (a c) -> p a c", a=16),
                            in_=st.H["v_sh"][r, :, h * 128:(h + 1) * 128]
                            .rearrange("(a p) c -> p a c", p=128)
                        ).then_inc(avl[p], 16)
                        st.bump(avl[p], 16)
                    vlt[h] = (avl[p], st.val(avl[p]))

            @blk.tensor
            def _(te_e):
                for m, (h, G, strips) in enumerate(groups):
                    p = m % 2
                    L = 4 * G + 4
                    te_e.wait_ge(*qkt[m])
                    te_e.wait_ge(*vlt[h])
                    if m >= 2:
                        te_e.wait_ge(arc, zz(arc) + m - 1)
                        te_e.wait_ge(aev, zz(aev) + m - 1)
                    npv = 0
                    for j, (r, l, smin2, masked) in enumerate(strips):
                        cj = cumstrip[m] + j
                        if cj >= 2:
                            te_e.wait_ge(aex, zz(aex) + cj - 1)
                        n = 512 - smin2 * 128
                        te_e.matmul(stp[cj % 2][:, smin2 * 128:512],
                                    k_sb[p][:, (r * L + l) * 128:(r * L + l + 1) * 128],
                                    q_sb[p][:, smin2 * 128:512],
                                    start=True, stop=True).then_inc(asm, 1)
                        st.bump(asm, 1)
                        if j >= 1:
                            npv = _attn_onespv(st, te_e, groups, m, j - 1,
                                               sums[p], pvp[p], v_sb[h % 2], pts,
                                               zz, cumstrip, npv, cummask)
                    npv = _attn_onespv(st, te_e, groups, m, len(strips) - 1,
                                       sums[p], pvp[p], v_sb[h % 2], pts,
                                       zz, cumstrip, npv, cummask)

            @blk.scalar
            def _(sa):
                for m, (h, G, strips) in enumerate(groups):
                    for j, (r, l, smin2, masked) in enumerate(strips):
                        cj = cumstrip[m] + j
                        sa.wait_ge(asm, zz(asm) + cj + 1)
                        if cj >= 2:
                            sa.wait_ge(aog, zz(aog) + cj - 1)
                        n = 512 - smin2 * 128
                        sa.activation(pts[cj % 2][:, smin2 * 128:512],
                                      stp[cj % 2][:, smin2 * 128:512],
                                      AF.Exp, bias=st.EXPB,
                                      scale=QK_SCALE).then_inc(aex, 1)
                        st.bump(aex, 1)

            @blk.vector
            def _(ve):
                ve.wait_ge(aml, st.val(aml))
                for m, (h, G, strips) in enumerate(groups):
                    p = m % 2
                    for j, (r, l, smin2, masked) in enumerate(strips):
                        cj = cumstrip[m] + j
                        if masked:
                            ve.wait_ge(aex, zz(aex) + cj + 1)
                            c0 = (l - 4 * G) * 128
                            ve.tensor_tensor(pts[cj % 2][:, c0:c0 + 128],
                                             pts[cj % 2][:, c0:c0 + 128],
                                             msk[:, r * 128:(r + 1) * 128],
                                             ALU.mult).then_inc(amk, 1)
                            st.bump(amk, 1)
                    ve.wait_ge(aog, zz(aog) + cumstrip[m] + len(strips))
                    ve.reciprocal(rinv[p][:, :], sums[p][:, :]).then_inc(arc, 1)
                    st.bump(arc, 1)
                    ve.wait_ge(arc, st.val(arc))
                    ve.tensor_tensor(
                        ao_sb[:, h * T + G * 512: h * T + (G + 1) * 512],
                        pvp[p][:, :], rinv[p][:, :], ALU.mult).then_inc(aev, 1)
                    st.bump(aev, 1)
                    if m == 0 and "atdbg" in st.taps:
                        dbg = st.dbg_sb
                        ve.tensor_copy(dbg[:, 0:512], sums[p][:, :])
                        ve.tensor_copy(dbg[:, 512:1024], pvp[p][:, :])
                        ve.tensor_copy(dbg[:, 1024:1536], rinv[p][:, :])
                        ve.tensor_copy(dbg[:, 1536:2048], pts[(cumstrip[0] + 7) % 2][:, :])
                        ve.tensor_copy(dbg[:, 2048:2560], q_sb[p][:, :]).then_inc(aev, 1)
                        st.bump(aev, 1)

        if "atdbg" in st.taps:
            dtp = st.sem("at_dtp")
            with nc.Block() as blk2:
                @blk2.gpsimd
                def _(g):
                    g.wait_ge(st.sems["at_aev"], st.val(st.sems["at_aev"]))
                    g.dma_start(out=st.taps["atdbg"].ap(),
                                in_=st.dbg_sb[:, :]).then_inc(dtp, 16)
                    g.wait_ge(dtp, st.val(dtp) + 16)
                    st.bump(dtp, 16)

        assert st.val(asm) == zz(asm) + NSTR
        assert st.val(aex) == zz(aex) + NSTR
        assert st.val(aog) == zz(aog) + NSTR
        assert st.val(arc) == zz(arc) + NM

    st.barrier_blk()


def _attn_onespv(st, te_e, groups, m, j, sums_ps, pv_ps, v_sb, pts, zz,
                 cumstrip, npv, cummask=None):
    ALU_ = ALU
    h, G, strips = groups[m]
    r, l, smin2, masked = strips[j]
    L = 4 * G + 4
    cj = cumstrip[m] + j
    aex = st.sems["at_aex"]
    amk = st.sems["at_amk"]
    aog = st.sems["at_aog"]
    te_e.wait_ge(aex, zz(aex) + cj + 1)
    if masked:
        nm = cummask[m] + sum(1 for t_ in strips[:j + 1] if t_[3])
        te_e.wait_ge(amk, zz(amk) + nm)
    n = 512 - smin2 * 128
    first = (j == 0)
    te_e.matmul(sums_ps[:, smin2 * 128:512], st.ones128[:, :],
                pts[cj % 2][:, smin2 * 128:512],
                start=first, stop=(j == len(strips) - 1))
    lastj = (j == len(strips) - 1)
    for si, s in enumerate(range(max(l, 4 * G), 4 * G + 4)):
        c0 = (s - 4 * G) * 128
        i = te_e.matmul(pv_ps[:, c0:c0 + 128],
                        v_sb[:, (r * 16 + l) * 128:(r * 16 + l + 1) * 128],
                        pts[cj % 2][:, c0:c0 + 128],
                        start=(first and si == 0), stop=lastj)
        npv += 1
    i.then_inc(aog, 1)
    st.bump(aog, 1)
    return npv


def phase_oproj(st, ctx, ao_sb):
    nc = st.nc
    olb = [st.sem("op_olb0"), st.sem("op_olb1")]
    xlb = [st.sem("op_xlb0"), st.sem("op_xlb1")]
    om = st.sem("op_om")
    oe = st.sem("op_oe")
    ost = [st.sem("op_ost0"), st.sem("op_ost1")]
    olt, xlt, ostt = {}, {}, {}
    z = {s.name: st.val(s) for s in (om, oe)}
    zo = [st.val(ost[0]), st.val(ost[1])]

    def zz(s):
        return z[s.name]

    with ExitStack() as pctx:
        e = pctx.enter_context
        wot = [e(nc.sbuf_tensor(f"op_wot{i}", [128, D], BF16)) for i in range(2)]
        x1t = [e(nc.sbuf_tensor(f"op_x1t{i}", [128, T], F32)) for i in range(2)]
        x2t = [e(nc.sbuf_tensor(f"op_x2t{i}", [128, T], F32)) for i in range(2)]
        ops = [e(nc.psum_tensor(f"op_ps{i}", [128, T], F32)) for i in range(2)]

        with nc.Block() as blk:
            @blk.gpsimd
            def _(g):
                for db in range(NDC):
                    p = db % 2
                    if db >= 2:
                        g.wait_ge(om, zz(om) + 4 * (db - 1))
                    g.dma_start(
                        out=wot[p][:, :].rearrange("p (a c) -> p a c", a=NDC),
                        in_=st.H["wo"][:, db * 128:(db + 1) * 128]
                        .rearrange("(a p) c -> p a c", p=128)
                    ).then_inc(olb[p], 16)
                    olt[db] = (olb[p], st.bump(olb[p], 16))

            @blk.sync
            def _(sy):
                def store(dbs):
                    ps = dbs % 2
                    sy.wait_ge(oe, zz(oe) + 4 * (dbs + 1))
                    sy.dma_start(out=st.H["x2"][dbs * 128:(dbs + 1) * 128, :],
                                 in_=x2t[ps][:, :]).then_inc(ost[ps], 16)
                    ostt[dbs] = (ost[ps], st.bump(ost[ps], 16))

                for db in range(NDC):
                    p = db % 2
                    if db >= 2:
                        sy.wait_ge(oe, zz(oe) + 4 * (db - 1))
                    sy.dma_start(out=x1t[p][:, :],
                                 in_=st.H["x1"][db * 128:(db + 1) * 128, :]
                                 ).then_inc(xlb[p], 16)
                    xlt[db] = (xlb[p], st.bump(xlb[p], 16))
                    if db >= 1:
                        store(db - 1)
                store(NDC - 1)
                for s_ in ost:
                    sy.wait_ge(s_, st.val(s_))

            @blk.tensor
            def _(te_e):
                for db in range(NDC):
                    p = db % 2
                    te_e.wait_ge(*olt[db])
                    if db >= 2:
                        te_e.wait_ge(oe, zz(oe) + 4 * (db - 1))
                    for tc in range(NTC):
                        i = None
                        for dv in range(NDC):
                            i = te_e.matmul(
                                ops[p][:, tc * 512:(tc + 1) * 512],
                                wot[p][:, dv * 128:(dv + 1) * 128],
                                ao_sb[:, dv * T + tc * 512: dv * T + (tc + 1) * 512],
                                start=(dv == 0), stop=(dv == NDC - 1))
                        i.then_inc(om, 1)
                        st.bump(om, 1)

            @blk.vector
            def _(ve):
                for db in range(NDC):
                    p = db % 2
                    ve.wait_ge(*xlt[db])
                    for tc in range(NTC):
                        ve.wait_ge(om, zz(om) + 4 * db + tc + 1)
                        if db >= 2:
                            ve.wait_ge(ost[p], zo[p] + 16 * (db // 2))
                        ve.tensor_tensor(
                            x2t[p][:, tc * 512:(tc + 1) * 512],
                            ops[p][:, tc * 512:(tc + 1) * 512],
                            x1t[p][:, tc * 512:(tc + 1) * 512],
                            ALU.add).then_inc(oe, 1)
                        st.bump(oe, 1)


        assert st.val(om) == zz(om) + 4 * NDC
        assert st.val(oe) == zz(oe) + 4 * NDC

    st.barrier_blk()


def _rope_tables():
    inv = 1.0 / 10000.0 ** (np.arange(0, HD, 2)[: HD // 2].astype(np.float32) / HD)
    t = np.arange(S)
    f = np.outer(t, inv).astype(np.float32)
    return np.cos(f), np.sin(f)


def _stage(inputs):
    x = np.asarray(inputs["L0_attn_x"])
    wq = np.asarray(inputs["L0_attn_wq"])
    wk = np.asarray(inputs["L0_attn_wk"])
    wv = np.asarray(inputs["L0_attn_wv"])
    wo = np.asarray(inputs["L0_attn_wo"])

    perm1 = np.concatenate([np.arange(0, HD, 2), np.arange(1, HD, 2)])
    perm = np.concatenate([h * HD + perm1 for h in range(H)])
    wq_p = np.ascontiguousarray(wq[:, perm])
    wk_p = np.ascontiguousarray(wk[:, perm])

    cosf, sinf = _rope_tables()

    shared = dict(
        wq=wq_p, wk=wk_p, wv=np.ascontiguousarray(wv), wo=np.ascontiguousarray(wo),
        pg=np.ascontiguousarray(np.asarray(inputs["L0_ffn_gate_Wq"], np.int32)),
        pu=np.ascontiguousarray(np.asarray(inputs["L0_ffn_up_Wq"], np.int32)),
        pd=np.ascontiguousarray(np.asarray(inputs["L0_ffn_down_Wq"], np.int32)),
        sg=np.ascontiguousarray(np.asarray(inputs["L0_ffn_gate_scale"], np.float32)),
        zg=np.ascontiguousarray(np.asarray(inputs["L0_ffn_gate_zero"], np.float32)),
        su=np.ascontiguousarray(np.asarray(inputs["L0_ffn_up_scale"], np.float32)),
        zu=np.ascontiguousarray(np.asarray(inputs["L0_ffn_up_zero"], np.float32)),
        sd=np.ascontiguousarray(np.asarray(inputs["L0_ffn_down_scale"], np.float32)),
        zd=np.ascontiguousarray(np.asarray(inputs["L0_ffn_down_zero"], np.float32)),
    )

    rng = np.random.default_rng()
    nonce = np.uint32(rng.integers(1, 2**31))

    tri = np.tril(np.ones((128, 128), np.float32)).T
    ones = np.ones((128, 128), np.float32)
    zeros = np.zeros((128, 128), np.float32)

    per_core = []
    for c in range(8):
        b, par = divmod(c, 2)
        cols = np.concatenate([np.arange(j * 128, (j + 1) * 128)
                               for j in range(par, 32, 2)])
        xs = np.ascontiguousarray(x[b].T[:, cols], dtype=np.float32)
        ct = cosf[cols].T
        stt = sinf[cols].T
        c128 = np.ascontiguousarray(np.concatenate([ct, ct], axis=0), np.float32)
        s128 = np.ascontiguousarray(np.concatenate([-stt, stt], axis=0), np.float32)
        if par == 0:
            masks = np.stack([tri, zeros])
        else:
            masks = np.stack([ones, tri])
        per_core.append(dict(
            x=xs, c128=c128, s128=s128,
            masks=masks.astype(ml_dtypes.bfloat16),
            nonce=np.array([[nonce]], np.uint32),
        ))
    return shared, per_core, cols


def build_program():
    nc = bass.Bass("TRN2", target_bir_lowering=False)

    x_h = nc.dram_tensor("x", [D, T], F32, kind="ExternalInput")
    wq_h = nc.dram_tensor("wq", [D, D], F32, kind="ExternalInput")
    wk_h = nc.dram_tensor("wk", [D, D], F32, kind="ExternalInput")
    wv_h = nc.dram_tensor("wv", [D, D], F32, kind="ExternalInput")
    wo_h = nc.dram_tensor("wo", [D, D], F32, kind="ExternalInput")
    pg_h = nc.dram_tensor("pg", [ID, HP], I32, kind="ExternalInput")
    pu_h = nc.dram_tensor("pu", [ID, HP], I32, kind="ExternalInput")
    pd_h = nc.dram_tensor("pd", [D, IP], I32, kind="ExternalInput")
    sg_h = nc.dram_tensor("sg", [ID, GRP], F32, kind="ExternalInput")
    zg_h = nc.dram_tensor("zg", [ID, GRP], F32, kind="ExternalInput")
    su_h = nc.dram_tensor("su", [ID, GRP], F32, kind="ExternalInput")
    zu_h = nc.dram_tensor("zu", [ID, GRP], F32, kind="ExternalInput")
    sd_h = nc.dram_tensor("sd", [D, IP // 32], F32, kind="ExternalInput")
    zd_h = nc.dram_tensor("zd", [D, IP // 32], F32, kind="ExternalInput")
    c128_h = nc.dram_tensor("c128", [128, T], F32, kind="ExternalInput")
    s128_h = nc.dram_tensor("s128", [128, T], F32, kind="ExternalInput")
    masks_h = nc.dram_tensor("masks", [2, 128, 128], BF16, kind="ExternalInput")
    nonce_h = nc.dram_tensor("nonce", [1, 1], U32, kind="ExternalInput")

    out_h = nc.dram_tensor("out", [D, T], F32, kind="ExternalOutput")

    x1_h = nc.dram_tensor("x1", [D, T], F32, kind="Internal")
    x2_h = nc.dram_tensor("x2", [D, T], F32, kind="Internal")
    hbuf_h = nc.dram_tensor("hbuf", [ID, T], BF16, kind="Internal")
    q_hb = nc.dram_tensor("qbuf", [D, T], BF16, kind="Internal")
    wgT_h = nc.dram_tensor("wgT", [D, ID], BF16, kind="Internal")
    wuT_h = nc.dram_tensor("wuT", [D, ID], BF16, kind="Internal")
    wdT_h = nc.dram_tensor("wdT", [ID, D], BF16, kind="Internal")
    k_loc = nc.dram_tensor("k_loc", [D, T], BF16, kind="Internal")
    v_loc = nc.dram_tensor("v_loc", [T, D], BF16, kind="Internal")
    k_sh = nc.dram_tensor("k_sh", [2, D, T], BF16, kind="Internal", addr_space="Shared")
    v_sh = nc.dram_tensor("v_sh", [2, T, D], BF16, kind="Internal", addr_space="Shared")
    flg_sh = nc.dram_tensor("flg_sh", [2, 1], U32, kind="Internal", addr_space="Shared")

    st = St(nc)
    st.H = dict(x=x_h, wq=wq_h, wk=wk_h, wv=wv_h, wo=wo_h,
                pg=pg_h, pu=pu_h, pd=pd_h, sg=sg_h, zg=zg_h, su=su_h, zu=zu_h,
                sd=sd_h, zd=zd_h, c128=c128_h, s128=s128_h, masks=masks_h,
                nonce=nonce_h, out=out_h, x1=x1_h, x2=x2_h, hbuf=hbuf_h,
                q=q_hb, wgT=wgT_h, wuT=wuT_h, wdT=wdT_h,
                k_loc=k_loc, v_loc=v_loc,
                k_sh=k_sh, v_sh=v_sh, flg=flg_sh)

    for tname, shape, dt in [("xn", [D, T], BF16), ("h", [ID, T], BF16),
                             ("wdTt", [ID, D], BF16),
                             ("x1t", [D, T], F32), ("qt", [D, T], BF16),
                             ("kt", [2, D, T], BF16), ("vt", [2, T, D], BF16),
                             ("aot", [D, T], BF16), ("x2t", [D, T], F32),
                             ("atdbg", [128, 2560], F32)]:
        if _taps_want(tname):
            st.taps[tname] = nc.dram_tensor("tap_" + tname, shape, dt,
                                            kind="ExternalOutput")

    def barrier_blk():
        with nc.Block() as blk:
            st.barrier(blk)
    st.barrier_blk = barrier_blk

    build_body(st)
    return nc, st


def _tap_out(st, name, sb, rows=None):
    if name not in st.taps:
        return
    nc = st.nc
    hdl = st.taps[name]
    nch = hdl.shape[0] // 128
    tp = st.sem("tap")
    with nc.Block() as blk:
        @blk.gpsimd
        def _(g):
            for ch in range(nch):
                g.dma_start(out=hdl[ch * 128:(ch + 1) * 128, :],
                            in_=sb[:, ch * T:(ch + 1) * T]).then_inc(tp, 16)
                st.bump(tp, 16)
            g.wait_ge(tp, st.val(tp))
    st.barrier_blk()


def build_body(st):
    nc = st.nc
    with ExitStack() as ctx:
        phase_init(st, ctx)

        with ExitStack() as span1:
            xn_sb = span1.enter_context(nc.sbuf_tensor("xn1s", [128, NDC * T], BF16))
            phase_ln(st, ctx, st.H["x"], xn_sb, "ln1")
            _tap_out(st, "xn", xn_sb)
            if PHASE_STOP == "ln1":
                finalize_stub(st)
                return
            phase_ffn_a(st, ctx, xn_sb, 0)
        if PHASE_STOP == "ffn1a":
            _tap_h(st)
            finalize_stub(st)
            return

        phase_wd(st, ctx)
        if PHASE_STOP == "wd":
            _tap_hbm(st, "wdTt", st.H["wdT"])
            finalize_stub(st)
            return
        phase_ffn_b(st, ctx, 0, st.H["x"], st.H["x1"])
        if PHASE_STOP == "ffn1":
            _tap_hbm(st, "x1t", st.H["x1"])
            finalize_stub(st)
            return

        with ExitStack() as span2:
            xn_sb = span2.enter_context(nc.sbuf_tensor("xn2s", [128, NDC * T], BF16))
            phase_ln(st, ctx, st.H["x1"], xn_sb, "ln2")
            phase_qkv(st, ctx, xn_sb)
        if PHASE_STOP == "qkv":
            _tap_hbm(st, "qt", st.H["q"])
            _tap_hbm(st, "kt", st.H["k_sh"])
            _tap_hbm(st, "vt", st.H["v_sh"])
            finalize_stub(st)
            return

        with ExitStack() as span3:
            ao_sb = span3.enter_context(nc.sbuf_tensor("ao_s", [128, NDC * T], BF16))
            phase_attn(st, ctx, ao_sb)
            if PHASE_STOP == "attn":
                _tap_out(st, "aot", ao_sb)
                finalize_stub(st)
                return
            phase_oproj(st, ctx, ao_sb)
        if PHASE_STOP == "oproj":
            _tap_hbm(st, "x2t", st.H["x2"])
            finalize_stub(st)
            return

        with ExitStack() as span4:
            xn_sb = span4.enter_context(nc.sbuf_tensor("xn3s", [128, NDC * T], BF16))
            phase_ln(st, ctx, st.H["x2"], xn_sb, "ln3")
            phase_ffn_a(st, ctx, xn_sb, 1)
        phase_ffn_b(st, ctx, 1, st.H["x2"], st.H["out"])


def _tap_hbm(st, name, src_h):
    if name not in st.taps:
        return
    nc = st.nc
    tp = st.sem("tap")
    with nc.Block() as blk:
        @blk.gpsimd
        def _(g):
            g.dma_start(out=st.taps[name].ap(), in_=src_h.ap()).then_inc(tp, 16)
            st.bump(tp, 16)
            g.wait_ge(tp, st.val(tp))
    st.barrier_blk()


def _tap_h(st):
    if "h" not in st.taps:
        return
    nc = st.nc
    tp = st.sem("tap")
    with nc.Block() as blk:
        @blk.gpsimd
        def _(g):
            g.dma_start(out=st.taps["h"].ap(), in_=st.H["hbuf"].ap()).then_inc(tp, 16)
            st.bump(tp, 16)
            g.wait_ge(tp, st.val(tp))
    st.barrier_blk()


def finalize_stub(st):
    nc = st.nc
    fz = st.sem("fin")
    with nc.Block() as blk:
        @blk.sync
        def _(sy):
            for dc in range(NDC):
                sy.dma_start(out=st.H["out"][dc * 128:(dc + 1) * 128, :],
                             in_=st.H["x"][dc * 128:(dc + 1) * 128, :]).then_inc(fz, 16)
                st.bump(fz, 16)
            sy.wait_ge(fz, st.val(fz))


def finalize_waits(st, outsem):
    nc = st.nc
    with nc.Block() as blk:
        @blk.sync
        def _(sy):
            sy.wait_ge(outsem, st.val(outsem))


_CACHE = {}


def kernel(**inputs):
    shared, per_core, _ = _stage(inputs)
    if "nc" not in _CACHE:
        _CACHE["nc"] = build_program()
    nc, st = _CACHE["nc"]
    in_maps = [dict(shared, **pc) for pc in per_core]
    res = run_bass_kernel_spmd(nc, in_maps, core_ids=list(range(8)))
    out = np.zeros((4, S, D), np.float32)
    for c in range(8):
        b, par = divmod(c, 2)
        o = res.results[c]["out"]
        cols = np.concatenate([np.arange(j * 128, (j + 1) * 128)
                               for j in range(par, 32, 2)])
        out[b][cols, :] = o.T
    if DEBUG_TAPS:
        kernel.last_taps = [{k: v for k, v in r.items() if k.startswith("tap_")}
                            for r in res.results]
    return out


# version 04 (final)
# speedup vs baseline: 1.2582x; 1.2582x over previous
import numpy as np
import ml_dtypes
from contextlib import ExitStack

import concourse.bass as bass
import concourse.mybir as mybir
from concourse.bass_utils import run_bass_kernel_spmd

F32 = mybir.dt.float32
BF16 = mybir.dt.bfloat16
I32 = mybir.dt.int32
U32 = mybir.dt.uint32
AF = mybir.ActivationFunctionType
ALU = mybir.AluOpType
AX = mybir.AxisListType

D = 4096
S = 4096
H = 32
HD = 128
ID = 11008
GRP = 64
T = 2048
NDC = D // 128
NTC = T // 512
NIB = ID // 128
HP = D // 2
IP = ID // 2
EXP_BIAS = -20.0
QK_SCALE = 1.0 / float(np.sqrt(HD))
LN_EPS = 1e-5

DEBUG_TAPS = []
PHASE_STOP = None
DEV_ATTN_GROUPS = None


def _taps_want(name):
    return name in DEBUG_TAPS


class St:
    def __init__(self, nc):
        self.nc = nc
        self.ct = {}
        self.sems = {}
        self.bar_n = 0
        self.taps = {}

    def sem(self, name):
        if name not in self.sems:
            hs = self.nc.semaphore(name).__enter__()
            self.sems[name] = hs
            self.ct[name] = 0
        return self.sems[name]

    def bump(self, sem, k=1):
        self.ct[sem.name] += k
        return self.ct[sem.name]

    def val(self, sem):
        return self.ct[sem.name]

    def dma(self, eng, out, in_, sem):
        eng.dma_start(out=out, in_=in_).then_inc(sem, 16)
        return self.bump(sem, 16)

    def barrier(self, blk):
        bar = self.sem("bar")
        self.bar_n += 1
        tgt = 5 * self.bar_n

        def mk(engname):
            def _(eng):
                eng.nop().then_inc(bar, 1)
                eng.wait_ge(bar, tgt)
            return _
        blk.sync(mk("sync"))
        blk.tensor(mk("tensor"))
        blk.scalar(mk("scalar"))
        blk.vector(mk("vector"))
        blk.gpsimd(mk("gpsimd"))
        self.ct["bar"] = tgt


def phase_init(st, ctx):
    nc = st.nc
    e = ctx.enter_context
    st.onesb = e(nc.sbuf_tensor("c_onesb", [128, 1], BF16))
    st.onesr = e(nc.sbuf_tensor("c_onesr", [1, 128], F32))
    st.ones128 = e(nc.sbuf_tensor("c_ones128", [128, 128], BF16))
    st.identb = e(nc.sbuf_tensor("c_identb", [128, 128], BF16))
    st.cvals = e(nc.sbuf_tensor("c_cvals", [128, 4], F32))
    st.EPS = st.cvals[:, 0:1]
    st.EXPB = st.cvals[:, 1:2]
    ic_ = st.sem("initc")
    with nc.Block() as blk:
        @blk.gpsimd
        def _(g):
            g.memset(st.onesb[:, :], 1.0)
            g.memset(st.onesr[:, :], 1.0)
            g.memset(st.cvals[:, 0:1], LN_EPS)
            g.memset(st.cvals[:, 1:2], EXP_BIAS)
            g.memset(st.ones128[:, :], 1.0).then_inc(ic_, 1)
            st.bump(ic_, 1)
            g.wait_ge(ic_, st.val(ic_))
            g.affine_select(st.identb[:, :], st.ones128[:, :], [[1, 128]],
                            ALU.is_equal, 0.0, channel_multiplier=-1)
    st.barrier_blk()


def phase_ln(st, ctx, src_h, xn_sb, tag):
    nc = st.nc
    ldb = [st.sem("ln_ldb0"), st.sem("ln_ldb1")]
    sc = st.sem("ln_sc")
    vc = st.sem("ln_vc")
    mm = st.sem("ln_mm")
    ldt = {}

    with ExitStack() as pctx:
        e = pctx.enter_context
        xt = [e(nc.sbuf_tensor(f"{tag}_xt{i}", [128, 512], F32)) for i in range(2)]
        xb = [e(nc.sbuf_tensor(f"{tag}_xb{i}", [128, 512], BF16)) for i in range(2)]
        sq = [e(nc.sbuf_tensor(f"{tag}_sq{i}", [128, 512], BF16)) for i in range(2)]
        stats = e(nc.sbuf_tensor(f"{tag}_stats", [1, 2048], F32))
        stats2 = e(nc.sbuf_tensor(f"{tag}_stats2", [1, 2048], F32))
        mvec = e(nc.sbuf_tensor(f"{tag}_m", [1, 2048], F32))
        rvec = e(nc.sbuf_tensor(f"{tag}_r", [1, 2048], F32))
        mr = e(nc.sbuf_tensor(f"{tag}_mr", [1, 2048], F32))
        tmp1 = e(nc.sbuf_tensor(f"{tag}_tmp", [1, 2048], F32))
        rbc = e(nc.sbuf_tensor(f"{tag}_rbc", [128, 2048], F32))
        mrbc = e(nc.sbuf_tensor(f"{tag}_mrbc", [128, 2048], F32))
        app = [e(nc.sbuf_tensor(f"{tag}_app{i}", [128, 512], F32)) for i in range(2)]
        sps = e(nc.psum_tensor(f"{tag}_sps", [1, 512], F32))
        qps = e(nc.psum_tensor(f"{tag}_qps", [1, 512], F32))
        bps = [e(nc.psum_tensor(f"{tag}_bps{i}", [128, 512], F32)) for i in range(2)]

        onesb = st.onesb
        onesr = st.onesr
        sc0 = st.val(sc)
        mm0 = st.val(mm)

        with nc.Block() as blk:
            @blk.sync
            def _(sy):
                n = 0
                for tc in range(NTC):
                    for dc in range(NDC):
                        p = n % 2
                        if n >= 2:
                            sy.wait_ge(sc, sc0 + 2 * (n - 1))
                        sy.dma_start(out=xt[p][:, :],
                                     in_=src_h[dc * 128:(dc + 1) * 128,
                                               tc * 512:(tc + 1) * 512]
                                     ).then_inc(ldb[p], 16)
                        ldt[n] = (ldb[p], st.bump(ldb[p], 16))
                        n += 1

            @blk.scalar
            def _(sa):
                n = 0
                for tc in range(NTC):
                    for dc in range(NDC):
                        p = n % 2
                        sa.wait_ge(*ldt[n])
                        if n >= 2:
                            sa.wait_ge(mm, mm0 + (n - 1))
                        sa.activation(sq[p][:, :], xt[p][:, :], AF.Square).then_inc(sc, 1)
                        sa.copy(xb[p][:, :], xt[p][:, :]).then_inc(sc, 1)
                        st.bump(sc, 2)
                        n += 1

            vc0 = st.val(vc)

            @blk.tensor
            def _(te):
                n = 0
                for tc in range(NTC):
                    if tc >= 1:
                        te.wait_ge(vc, vc0 + tc)
                    for dc in range(NDC):
                        p = n % 2
                        te.wait_ge(sc, sc0 + 2 * (n + 1))
                        te.matmul(sps[:, :], onesb[:, :], xb[p][:, :],
                                  start=(dc == 0), stop=(dc == NDC - 1))
                        te.matmul(qps[:, :], onesb[:, :], sq[p][:, :],
                                  start=(dc == 0), stop=(dc == NDC - 1)
                                  ).then_inc(mm, 1)
                        st.bump(mm, 1)
                        n += 1

            @blk.vector
            def _(ve):
                for tc in range(NTC):
                    ve.wait_ge(mm, mm0 + (tc + 1) * NDC)
                    ve.tensor_copy(stats[:, tc * 512:(tc + 1) * 512], sps[:, :])
                    ve.tensor_copy(stats2[:, tc * 512:(tc + 1) * 512], qps[:, :]).then_inc(vc, 1)
                    st.bump(vc, 1)
                ve.wait_ge(vc, st.val(vc))
                ve.tensor_scalar_mul(mvec[:, :], stats[:, :], 1.0 / D).then_inc(vc, 1)
                st.bump(vc, 1)
                ve.tensor_scalar_mul(tmp1[:, :], stats2[:, :], 1.0 / D).then_inc(vc, 1)
                st.bump(vc, 1)
                ve.wait_ge(vc, st.val(vc))
                ve.tensor_tensor(rvec[:, :], mvec[:, :], mvec[:, :], ALU.mult).then_inc(vc, 1)
                st.bump(vc, 1)
                ve.wait_ge(vc, st.val(vc))
                ve.tensor_tensor(tmp1[:, :], tmp1[:, :], rvec[:, :], ALU.subtract).then_inc(vc, 1)
                st.bump(vc, 1)

            vc_var = st.val(vc)

            @blk.scalar
            def _(sa):
                sa.wait_ge(vc, vc_var)
                sa.activation(tmp1[:, :], tmp1[:, :], AF.Sqrt, bias=st.EPS[0:1, :]).then_inc(sc, 1)
                st.bump(sc, 1)

            sc_sqrt = st.val(sc)

            @blk.vector
            def _(ve):
                ve.wait_ge(sc, sc_sqrt)
                ve.reciprocal(rvec[:, :], tmp1[:, :]).then_inc(vc, 1)
                st.bump(vc, 1)
                ve.wait_ge(vc, st.val(vc))
                ve.tensor_tensor(mr[:, :], mvec[:, :], rvec[:, :], ALU.mult).then_inc(vc, 1)
                st.bump(vc, 1)

            vc_mr = st.val(vc)
            mm_bc0 = st.val(mm)
            sc_bc0 = st.val(sc)

            @blk.tensor
            def _(te):
                te.wait_ge(vc, vc_mr)
                for tc in range(NTC):
                    if tc >= 1:
                        te.wait_ge(sc, sc_bc0 + tc)
                    te.matmul(bps[0][:, :], onesr[:, :],
                              rvec[0:1, tc * 512:(tc + 1) * 512],
                              start=True, stop=True)
                    te.matmul(bps[1][:, :], onesr[:, :],
                              mr[0:1, tc * 512:(tc + 1) * 512],
                              start=True, stop=True).then_inc(mm, 1)
                    st.bump(mm, 1)

            @blk.scalar
            def _(sa):
                for tc in range(NTC):
                    sa.wait_ge(mm, mm_bc0 + tc + 1)
                    sa.copy(rbc[:, tc * 512:(tc + 1) * 512], bps[0][:, :])
                    sa.copy(mrbc[:, tc * 512:(tc + 1) * 512], bps[1][:, :]).then_inc(sc, 1)
                    st.bump(sc, 1)

            sc1 = st.val(sc)
            vc1 = st.val(vc)

            @blk.sync
            def _(sy):
                n = 0
                for dc in range(NDC):
                    for tc in range(NTC):
                        p = n % 2
                        if n < 2:
                            sy.wait_ge(sc, sc1)
                        else:
                            sy.wait_ge(vc, vc1 + (n - 1) * 2)
                        sy.dma_start(out=xt[p][:, :],
                                     in_=src_h[dc * 128:(dc + 1) * 128,
                                               tc * 512:(tc + 1) * 512]
                                     ).then_inc(ldb[p], 16)
                        ldt[1000 + n] = (ldb[p], st.bump(ldb[p], 16))
                        n += 1

            @blk.vector
            def _(ve):
                ve.wait_ge(sc, sc1)
                n = 0
                vc2 = st.val(vc)
                for dc in range(NDC):
                    for tc in range(NTC):
                        p = n % 2
                        ve.wait_ge(*ldt[1000 + n])
                        if n >= 2:
                            ve.wait_ge(vc, vc2 + 2 * (n - 1))
                        ve.tensor_tensor(app[p][:, :], xt[p][:, :],
                                         rbc[:, tc * 512:(tc + 1) * 512], ALU.mult).then_inc(vc, 1)
                        st.bump(vc, 1)
                        ve.wait_ge(vc, st.val(vc))
                        ve.tensor_tensor(
                            xn_sb[:, dc * T + tc * 512: dc * T + (tc + 1) * 512],
                            app[p][:, :], mrbc[:, tc * 512:(tc + 1) * 512],
                            ALU.subtract).then_inc(vc, 1)
                        st.bump(vc, 1)
                        n += 1

    st.barrier_blk()


def phase_ffn_a(st, ctx, xn_sb, pass_):
    nc = st.nc
    tagp = f"fa{pass_}"
    sem_tag = "fa"
    pkb = [st.sem(sem_tag + "_pkb0"), st.sem(sem_tag + "_pkb1")]
    slb = [st.sem(sem_tag + "_slb0"), st.sem(sem_tag + "_slb1")]
    wtb = [st.sem(sem_tag + "_wtb%d" % i) for i in range(3)]
    cwb = [st.sem(sem_tag + "_cwb%d" % i) for i in range(3)]
    pkt, slt, wtt, cwt = {}, {}, {}, {}
    vq = st.sem(sem_tag + "_vq")
    sq = st.sem(sem_tag + "_sq")
    dqv = st.sem(sem_tag + "_dqv")
    tt = st.sem(sem_tag + "_tt")
    te = st.sem(sem_tag + "_te")
    mmg = st.sem(sem_tag + "_mmg")
    sil = st.sem(sem_tag + "_sil")
    hv = st.sem(sem_tag + "_hv")
    hst = st.sem(sem_tag + "_hst")

    z = {s.name: st.val(s) for s in (vq, sq, dqv, tt, te, mmg, sil, hv, hst)}

    def zz(s):
        return z[s.name]

    with ExitStack() as pctx:
        e = pctx.enter_context
        WT = [e(nc.sbuf_tensor(f"{tagp}_wt{i}", [128, D], BF16)) for i in range(3)]
        sg_sb = [e(nc.sbuf_tensor(f"{tagp}_sg{i}", [128, 1024], F32)) for i in range(2)]
        h_sb = e(nc.sbuf_tensor(f"{tagp}_h", [128, T], BF16))
        gu_ps = [e(nc.psum_tensor(f"{tagp}_gu{i}", [128, 1024], F32)) for i in range(2)]
        if pass_ == 0:
            pk = [e(nc.sbuf_tensor(f"{tagp}_pk{i}", [128, HP], I32)) for i in range(2)]
            scl = [e(nc.sbuf_tensor(f"{tagp}_scl{i}", [128, GRP], F32)) for i in range(4)]
            hi_i = e(nc.sbuf_tensor(f"{tagp}_hii", [128, 512], I32))
            lo_i = e(nc.sbuf_tensor(f"{tagp}_loi", [128, 512], I32))
            hi_f = e(nc.sbuf_tensor(f"{tagp}_hif", [128, 512], F32))
            lo_f = e(nc.sbuf_tensor(f"{tagp}_lof", [128, 512], F32))
            wnat = [e(nc.sbuf_tensor(f"{tagp}_wnat{i}", [128, D], BF16))
                    for i in range(2)]
            tp_ps = [e(nc.psum_tensor(f"{tagp}_tp{i}", [128, 512], BF16))
                     for i in range(2)]

        NSEQ = 2 * NIB
        pksrc = [st.H["pg"], st.H["pu"]]
        sclsrc = [(st.H["sg"], st.H["zg"]), (st.H["su"], st.H["zu"])]
        wtsrc = [st.H["wgT"], st.H["wuT"]]

        with nc.Block() as blk:
            @blk.sync
            def _(sy):
                if pass_ == 0:
                    for seq in range(NSEQ):
                        ib, wi = divmod(seq, 2)
                        p = seq % 2
                        if seq >= 2:
                            sy.wait_ge(vq, zz(vq) + 12 * (seq - 1))
                        sy.dma_start(out=pk[p][:, :],
                                     in_=pksrc[wi][ib * 128:(ib + 1) * 128, :]
                                     ).then_inc(pkb[p], 16)
                        pkt[seq] = (pkb[p], st.bump(pkb[p], 16))
                else:
                    for seq in range(NSEQ):
                        ib, wi = divmod(seq, 2)
                        b = seq % 3
                        if seq >= 3:
                            sprev = seq - 3
                            ibp, wip = divmod(sprev, 2)
                            sy.wait_ge(mmg, zz(mmg) + 4 * ibp + 2 * wip + 2)
                        sy.dma_start(
                            out=WT[b][:, :].rearrange("p (a c) -> p a c", a=NDC),
                            in_=wtsrc[wi][:, ib * 128:(ib + 1) * 128]
                            .rearrange("(a p) c -> p a c", p=128)
                        ).then_inc(wtb[b], 16)
                        wtt[seq] = (wtb[b], st.bump(wtb[b], 16))

            def emit_cw(g, sc):
                ibc, wic = divmod(sc, 2)
                g.wait_ge(te, zz(te) + 8 * (sc + 1))
                g.dma_start(
                    out=wtsrc[wic][:, ibc * 128:(ibc + 1) * 128]
                    .rearrange("(a p) c -> p a c", p=128),
                    in_=WT[sc % 3][:, :].rearrange("p (a c) -> p a c", a=NDC)
                ).then_inc(cwb[sc % 3], 16)
                cwt[sc] = (cwb[sc % 3], st.bump(cwb[sc % 3], 16))

            def emit_hstore(g, ib):
                g.wait_ge(hv, zz(hv) + 2 * ib + 2)
                g.dma_start(out=st.H["hbuf"][ib * 128:(ib + 1) * 128, :],
                            in_=h_sb[:, :]).then_inc(hst, 16)
                st.bump(hst, 16)

            @blk.gpsimd
            def _(g):
                if pass_ == 0:
                    for seq in range(NSEQ):
                        ib, wi = divmod(seq, 2)
                        p = seq % 2
                        if seq >= 2:
                            g.wait_ge(dqv, zz(dqv) + seq - 1)
                        g.dma_start(out=scl[2 * p][:, :],
                                    in_=sclsrc[wi][0][ib * 128:(ib + 1) * 128, :]
                                    ).then_inc(slb[p], 16)
                        st.bump(slb[p], 16)
                        g.dma_start(out=scl[2 * p + 1][:, :],
                                    in_=sclsrc[wi][1][ib * 128:(ib + 1) * 128, :]
                                    ).then_inc(slb[p], 16)
                        slt[seq] = (slb[p], st.bump(slb[p], 16))
                        if seq >= 2:
                            emit_cw(g, seq - 2)
                        if wi == 0 and seq >= 2:
                            emit_hstore(g, seq // 2 - 1)
                    emit_hstore(g, NIB - 1)
                    emit_cw(g, NSEQ - 2)
                    emit_cw(g, NSEQ - 1)
                else:
                    for ib in range(NIB):
                        emit_hstore(g, ib)
                g.wait_ge(hst, st.val(hst))
                if pass_ == 0:
                    for s_ in cwb:
                        g.wait_ge(s_, st.val(s_))

            def emit_deq(ve, seq):
                ib, wi = divmod(seq, 2)
                p = seq % 2
                ve.wait_ge(*pkt[seq])
                ve.wait_ge(*slt[seq])
                if seq >= 2:
                    ve.wait_ge(tt, zz(tt) + 32 * (seq - 1))
                sct = scl[2 * p][:, :].unsqueeze(2)
                zct = scl[2 * p + 1][:, :].unsqueeze(2)
                for q in range(4):
                    if seq + q > 0:
                        ve.wait_ge(sq, zz(sq) + 4 * seq + q)
                    ve.tensor_single_scalar(hi_i[:, :], pk[p][:, q * 512:(q + 1) * 512],
                                            4, ALU.logical_shift_right).then_inc(vq, 1)
                    st.bump(vq, 1)
                    ve.wait_ge(vq, st.val(vq))
                    ve.tensor_single_scalar(lo_i[:, :], pk[p][:, q * 512:(q + 1) * 512],
                                            15, ALU.bitwise_and).then_inc(vq, 1)
                    st.bump(vq, 1)
                    ve.wait_ge(sq, zz(sq) + 4 * seq + q + 1)
                    zb = zct[:, q * 16:(q + 1) * 16, :].broadcast_to([128, 16, 32])
                    sb = sct[:, q * 16:(q + 1) * 16, :].broadcast_to([128, 16, 32])
                    h3 = hi_f[:, :].rearrange("p (a b) -> p a b", a=16)
                    l3 = lo_f[:, :].rearrange("p (a b) -> p a b", a=16)
                    ve.tensor_tensor(h3, h3, zb, ALU.subtract)
                    ve.tensor_tensor(l3, l3, zb, ALU.subtract).then_inc(vq, 1)
                    st.bump(vq, 1)
                    ve.wait_ge(vq, st.val(vq))
                    wn2 = wnat[wi][:, q * 1024:(q + 1) * 1024].rearrange(
                        "p (a two) -> p a two", two=2)
                    wn_e = wn2[:, :, 0:1].squeeze(2).rearrange("p (a b) -> p a b", a=16)
                    wn_o = wn2[:, :, 1:2].squeeze(2).rearrange("p (a b) -> p a b", a=16)
                    ve.tensor_tensor(wn_e, h3, sb, ALU.mult)
                    i = ve.tensor_tensor(wn_o, l3, sb, ALU.mult)
                    if q == 3:
                        i.then_inc(dqv, 1)
                        st.bump(dqv, 1)

            def emit_evac(ve, seq):
                b = seq % 3
                for g in range(8):
                    ve.wait_ge(tt, zz(tt) + 32 * seq + 4 * (g + 1))
                    ve.tensor_copy(WT[b][:, g * 512:(g + 1) * 512],
                                   tp_ps[g % 2][:, :]).then_inc(te, 1)
                    st.bump(te, 1)

            def emit_htt(ve, ib):
                for th in range(2):
                    ve.wait_ge(mmg, zz(mmg) + 4 * ib + 2 + th + 1)
                    ve.wait_ge(sil, zz(sil) + 2 * ib + th + 1)
                    if ib >= 1:
                        ve.wait_ge(hst, zz(hst) + 16 * ib)
                    ve.tensor_tensor(h_sb[:, th * 1024:(th + 1) * 1024],
                                     sg_sb[th][:, :], gu_ps[th][:, :],
                                     ALU.mult).then_inc(hv, 1)
                    st.bump(hv, 1)

            @blk.vector
            def _(ve):
                if pass_ == 0:
                    emit_deq(ve, 0)
                    emit_deq(ve, 1)
                    emit_evac(ve, 0)
                    emit_evac(ve, 1)
                    for ib in range(NIB):
                        if ib + 1 < NIB:
                            emit_deq(ve, 2 * ib + 2)
                            emit_evac(ve, 2 * ib + 2)
                        emit_htt(ve, ib)
                        if ib + 1 < NIB:
                            emit_deq(ve, 2 * ib + 3)
                            emit_evac(ve, 2 * ib + 3)
                else:
                    for ib in range(NIB):
                        emit_htt(ve, ib)

            @blk.scalar
            def _(sa):
                def cvts(seq):
                    for q in range(4):
                        sa.wait_ge(vq, zz(vq) + 12 * seq + 3 * q + 2)
                        sa.copy(hi_f[:, :], hi_i[:, :])
                        sa.copy(lo_f[:, :], lo_i[:, :]).then_inc(sq, 1)
                        st.bump(sq, 1)

                def silu_(ib, th):
                    sa.wait_ge(mmg, zz(mmg) + 4 * ib + th + 1)
                    if ib >= 1:
                        sa.wait_ge(hv, zz(hv) + 2 * (ib - 1) + th + 1)
                    sa.activation(sg_sb[th][:, :], gu_ps[th][:, :],
                                  AF.Silu).then_inc(sil, 1)
                    st.bump(sil, 1)

                if pass_ == 0:
                    cvts(0)
                    cvts(1)
                    for ib in range(NIB):
                        if ib + 1 < NIB:
                            cvts(2 * ib + 2)
                        silu_(ib, 0)
                        silu_(ib, 1)
                        if ib + 1 < NIB:
                            cvts(2 * ib + 3)
                else:
                    for ib in range(NIB):
                        silu_(ib, 0)
                        silu_(ib, 1)

            def emit_T(te_e, seq):
                ib, wi = divmod(seq, 2)
                te_e.wait_ge(dqv, zz(dqv) + seq + 1)
                if seq >= 3:
                    sprev = seq - 3
                    ibp, wip = divmod(sprev, 2)
                    te_e.wait_ge(mmg, zz(mmg) + 4 * ibp + 2 * wip + 2)
                    te_e.wait_ge(*cwt[sprev])
                for hc in range(NDC):
                    g = hc // 4
                    if hc % 4 == 0 and 8 * seq + g >= 2:
                        te_e.wait_ge(te, zz(te) + 8 * seq + g - 1)
                    te_e.matmul(tp_ps[g % 2][:, (hc % 4) * 128:(hc % 4 + 1) * 128],
                                wnat[wi][:, hc * 128:(hc + 1) * 128],
                                st.identb[:, :], start=(hc % 4 == 0),
                                stop=(hc % 4 == 3),
                                is_transpose=True).then_inc(tt, 1)
                    st.bump(tt, 1)

            def emit_mm(te_e, seq):
                ib, wi = divmod(seq, 2)
                b = seq % 3
                if pass_ == 0:
                    wt_ready = (te, zz(te) + 8 * (seq + 1))
                else:
                    wt_ready = wtt[seq]
                for th in range(2):
                    te_e.wait_ge(*wt_ready)
                    if wi == 0:
                        if ib >= 1:
                            te_e.wait_ge(hv, zz(hv) + 2 * (ib - 1) + th + 1)
                    else:
                        te_e.wait_ge(sil, zz(sil) + 2 * ib + th + 1)
                    for tc2 in range(2):
                        col = th * 1024 + tc2 * 512
                        for hc in range(NDC):
                            i = te_e.matmul(
                                gu_ps[th][:, tc2 * 512:(tc2 + 1) * 512],
                                WT[b][:, hc * 128:(hc + 1) * 128],
                                xn_sb[:, hc * T + col: hc * T + col + 512],
                                start=(hc == 0), stop=(hc == NDC - 1))
                    i.then_inc(mmg, 1)
                    st.bump(mmg, 1)

            @blk.tensor
            def _(te_e):
                if pass_ == 0:
                    emit_T(te_e, 0)
                    emit_T(te_e, 1)
                    for ib in range(NIB):
                        emit_mm(te_e, 2 * ib)
                        if ib + 1 < NIB:
                            emit_T(te_e, 2 * ib + 2)
                        emit_mm(te_e, 2 * ib + 1)
                        if ib + 1 < NIB:
                            emit_T(te_e, 2 * ib + 3)
                else:
                    for ib in range(NIB):
                        emit_mm(te_e, 2 * ib)
                        emit_mm(te_e, 2 * ib + 1)

        if pass_ == 0:
            assert st.val(vq) == zz(vq) + 12 * NSEQ
            assert st.val(sq) == zz(sq) + 4 * NSEQ
            assert st.val(dqv) == zz(dqv) + NSEQ
            assert st.val(tt) == zz(tt) + 32 * NSEQ
            assert st.val(te) == zz(te) + 8 * NSEQ
        assert st.val(mmg) == zz(mmg) + 4 * NIB
        assert st.val(sil) == zz(sil) + 2 * NIB
        assert st.val(hv) == zz(hv) + 2 * NIB
        assert st.val(hst) == zz(hst) + 16 * NIB

    st.barrier_blk()


def phase_wd(st, ctx):
    nc = st.nc
    ldb = [st.sem("wd_ldb0"), st.sem("wd_ldb1")]
    slb = [st.sem("wd_slb0"), st.sem("wd_slb1")]
    ldt, slt = {}, {}
    vq = st.sem("wd_vq")
    sq = st.sem("wd_sq")
    dqv = st.sem("wd_dqv")
    tt = st.sem("wd_tt")
    te = st.sem("wd_te")
    cw = st.sem("wd_cw")
    QW = IP // 4
    NCH = 8
    chunks = [(12 * i, min(12, NIB - 12 * i)) for i in range(NCH)]
    NGR = 22
    ngrp = [(4 * g, min(4, NIB - 4 * g)) for g in range(NGR)]

    z = {s.name: st.val(s) for s in (vq, sq, dqv, tt, te, cw)}

    def zz(s):
        return z[s.name]

    with ExitStack() as pctx:
        e = pctx.enter_context
        pq = [e(nc.sbuf_tensor(f"wd_pq{i}", [128, QW], I32)) for i in range(2)]
        scl = [e(nc.sbuf_tensor(f"wd_scl{i}", [128, IP // 32], F32)) for i in range(4)]
        hi_i = e(nc.sbuf_tensor("wd_hii", [128, QW], I32))
        lo_i = e(nc.sbuf_tensor("wd_loi", [128, QW], I32))
        hi_f = e(nc.sbuf_tensor("wd_hif", [128, QW], F32))
        lo_f = e(nc.sbuf_tensor("wd_lof", [128, QW], F32))
        wnat = e(nc.sbuf_tensor("wd_wnat", [128, ID], BF16))
        wpart = e(nc.sbuf_tensor("wd_part", [128, 12 * 128], BF16))
        tp_ps = [e(nc.psum_tensor(f"wd_tp{i}", [128, 512], BF16)) for i in range(2)]

        with nc.Block() as blk:
            @blk.sync
            def _(sy):
                for db in range(NDC):
                    for q in range(4):
                        qs = db * 4 + q
                        p = qs % 2
                        if qs >= 2:
                            sy.wait_ge(vq, zz(vq) + 3 * (qs - 1))
                        sy.dma_start(out=pq[p][:, :],
                                     in_=st.H["pd"][db * 128:(db + 1) * 128,
                                                    q * QW:(q + 1) * QW]
                                     ).then_inc(ldb[p], 16)
                        ldt[qs] = (ldb[p], st.bump(ldb[p], 16))

            @blk.gpsimd
            def _(g):
                for db in range(NDC):
                    p = db % 2
                    if db >= 2:
                        g.wait_ge(dqv, zz(dqv) + 4 * (db - 1))
                    g.dma_start(out=scl[2 * p][:, :],
                                in_=st.H["sd"][db * 128:(db + 1) * 128, :]
                                ).then_inc(slb[p], 16)
                    st.bump(slb[p], 16)
                    g.dma_start(out=scl[2 * p + 1][:, :],
                                in_=st.H["zd"][db * 128:(db + 1) * 128, :]
                                ).then_inc(slb[p], 16)
                    slt[db] = (slb[p], st.bump(slb[p], 16))
                    for ci, (c0, cnt) in enumerate(chunks):
                        g.wait_ge(te, zz(te) + NGR * db + (c0 + cnt + 3) // 4)
                        g.dma_start(
                            out=st.H["wdT"][c0 * 128:(c0 + cnt) * 128,
                                            db * 128:(db + 1) * 128]
                            .rearrange("(a p) c -> p a c", p=128),
                            in_=wpart[:, :cnt * 128].rearrange(
                                "p (a c) -> p a c", a=cnt)
                        ).then_inc(cw, 16)
                        st.bump(cw, 16)
                g.wait_ge(cw, st.val(cw))

            @blk.vector
            def _(ve):
                for db in range(NDC):
                    p = db % 2
                    sct = scl[2 * p][:, :].unsqueeze(2)
                    zct = scl[2 * p + 1][:, :].unsqueeze(2)
                    for q in range(4):
                        qs = db * 4 + q
                        ve.wait_ge(*ldt[qs])
                        ve.wait_ge(*slt[db])
                        if q == 0 and db >= 1:
                            ve.wait_ge(tt, zz(tt) + NIB * db)
                        if qs > 0:
                            ve.wait_ge(sq, zz(sq) + qs)
                        ve.tensor_single_scalar(hi_i[:, :], pq[qs % 2][:, :],
                                                4, ALU.logical_shift_right).then_inc(vq, 1)
                        st.bump(vq, 1)
                        ve.wait_ge(vq, st.val(vq))
                        ve.tensor_single_scalar(lo_i[:, :], pq[qs % 2][:, :],
                                                15, ALU.bitwise_and).then_inc(vq, 1)
                        st.bump(vq, 1)
                        ve.wait_ge(sq, zz(sq) + qs + 1)
                        ng = QW // 32
                        zb = zct[:, q * ng:(q + 1) * ng, :].broadcast_to([128, ng, 32])
                        sb = sct[:, q * ng:(q + 1) * ng, :].broadcast_to([128, ng, 32])
                        h3 = hi_f[:, :].rearrange("p (a b) -> p a b", a=ng)
                        l3 = lo_f[:, :].rearrange("p (a b) -> p a b", a=ng)
                        ve.tensor_tensor(h3, h3, zb, ALU.subtract)
                        ve.tensor_tensor(l3, l3, zb, ALU.subtract).then_inc(vq, 1)
                        st.bump(vq, 1)
                        ve.wait_ge(vq, st.val(vq))
                        wn2 = wnat[:, q * 2 * QW:(q + 1) * 2 * QW].rearrange(
                            "p (a two) -> p a two", two=2)
                        wn_e = wn2[:, :, 0:1].squeeze(2).rearrange("p (a b) -> p a b", a=ng)
                        wn_o = wn2[:, :, 1:2].squeeze(2).rearrange("p (a b) -> p a b", a=ng)
                        ve.tensor_tensor(wn_e, h3, sb, ALU.mult)
                        ve.tensor_tensor(wn_o, l3, sb, ALU.mult).then_inc(dqv, 1)
                        st.bump(dqv, 1)
                    for gi, (g0, gcnt) in enumerate(ngrp):
                        ve.wait_ge(tt, zz(tt) + NIB * db + g0 + gcnt)
                        ci = min(g0 // 12, NCH - 1)
                        c0 = chunks[ci][0]
                        if g0 == c0 and (db > 0 or gi > 0):
                            nprev = NCH * db + ci
                            ve.wait_ge(cw, zz(cw) + 16 * nprev)
                        ve.tensor_copy(wpart[:, (g0 - c0) * 128:(g0 - c0 + gcnt) * 128],
                                       tp_ps[gi % 2][:, :gcnt * 128]).then_inc(te, 1)
                        st.bump(te, 1)

            @blk.scalar
            def _(sa):
                for db in range(NDC):
                    for q in range(4):
                        qs = db * 4 + q
                        sa.wait_ge(vq, zz(vq) + 3 * qs + 2)
                        sa.copy(hi_f[:, :], hi_i[:, :])
                        sa.copy(lo_f[:, :], lo_i[:, :]).then_inc(sq, 1)
                        st.bump(sq, 1)

            @blk.tensor
            def _(te_e):
                for db in range(NDC):
                    te_e.wait_ge(dqv, zz(dqv) + 4 * (db + 1))
                    for ic in range(NIB):
                        gi = ic // 4
                        gcnt = ngrp[gi][1]
                        if ic % 4 == 0 and NGR * db + gi >= 2:
                            te_e.wait_ge(te, zz(te) + NGR * db + gi - 1)
                        te_e.matmul(tp_ps[gi % 2][:, (ic % 4) * 128:(ic % 4 + 1) * 128],
                                    wnat[:, ic * 128:(ic + 1) * 128],
                                    st.identb[:, :], start=(ic % 4 == 0),
                                    stop=(ic % 4 == gcnt - 1),
                                    is_transpose=True).then_inc(tt, 1)
                        st.bump(tt, 1)

        assert st.val(vq) == zz(vq) + 12 * NDC
        assert st.val(dqv) == zz(dqv) + 4 * NDC
        assert st.val(tt) == zz(tt) + NIB * NDC
        assert st.val(te) == zz(te) + NGR * NDC
        assert st.val(cw) == zz(cw) + 16 * NCH * NDC

    st.barrier_blk()


def phase_ffn_b(st, ctx, pass_, src_h, dst_h):
    nc = st.nc
    tagp = f"fb{pass_}"
    hl = st.sem("fb_hl")
    wlb = [st.sem("fb_wlb0"), st.sem("fb_wlb1")]
    wlt = {}
    xl = st.sem("fb_xl")
    mmh = st.sem("fb_mmh")
    ev = st.sem("fb_ev")
    xs = st.sem("fb_xs")

    z = {s.name: st.val(s) for s in (hl, xl, mmh, ev, xs)}

    def zz(s):
        return z[s.name]

    NGEN = 2 * NDC

    with ExitStack() as pctx:
        e = pctx.enter_context
        hres = e(nc.sbuf_tensor(f"{tagp}_hres", [128, NIB * 1024], BF16))
        wdt = [e(nc.sbuf_tensor(f"{tagp}_wdt{i}", [128, 43 * 128], BF16))
               for i in range(2)]
        xres = [e(nc.sbuf_tensor(f"{tagp}_x0", [128, 1024], F32))]
        od = [e(nc.sbuf_tensor(f"{tagp}_o0", [128, 1024], F32))]
        yps = [e(nc.psum_tensor(f"{tagp}_y{i}", [128, 1024], F32)) for i in range(2)]

        with nc.Block() as blk:
            @blk.gpsimd
            def _(g):
                for th in range(2):
                    if th == 1:
                        g.wait_ge(mmh, zz(mmh) + 2 * NDC)
                    for ic in range(NIB):
                        g.dma_start(out=hres[:, ic * 1024:(ic + 1) * 1024],
                                    in_=st.H["hbuf"][ic * 128:(ic + 1) * 128,
                                                     th * 1024:(th + 1) * 1024]
                                    ).then_inc(hl, 16)
                        st.bump(hl, 16)
                    for db in range(NDC):
                        gen = th * NDC + db
                        if gen >= 1:
                            g.wait_ge(ev, zz(ev) + gen)
                        g.dma_start(out=xres[0][:, :],
                                    in_=src_h[db * 128:(db + 1) * 128,
                                              th * 1024:(th + 1) * 1024]
                                    ).then_inc(xl, 16)
                        st.bump(xl, 16)
                        g.wait_ge(ev, zz(ev) + gen + 1)
                        g.dma_start(out=dst_h[db * 128:(db + 1) * 128,
                                              th * 1024:(th + 1) * 1024],
                                    in_=od[0][:, :]).then_inc(xs, 16)
                        st.bump(xs, 16)
                g.wait_ge(xs, st.val(xs))

            @blk.sync
            def _(sy):
                subs = [(0, 11), (11, 11), (22, 11), (33, 10)]
                for th in range(2):
                    for db in range(NDC):
                        gen = th * NDC + db
                        for hf in range(2):
                            if gen > 0 or hf > 0:
                                sy.wait_ge(mmh, zz(mmh) + 2 * gen + hf - 1)
                            for (s0, scnt) in subs:
                                r0 = (hf * 43 + s0) * 128
                                sy.dma_start(
                                    out=wdt[hf][:, s0 * 128:(s0 + scnt) * 128]
                                    .rearrange("p (a c) -> p a c", a=scnt),
                                    in_=st.H["wdT"][r0:r0 + scnt * 128,
                                                    db * 128:(db + 1) * 128]
                                    .rearrange("(a p) c -> p a c", p=128)
                                ).then_inc(wlb[hf], 16)
                                wlt[(gen, hf)] = (wlb[hf], st.bump(wlb[hf], 16))

            @blk.tensor
            def _(te_e):
                for th in range(2):
                    te_e.wait_ge(hl, zz(hl) + 16 * NIB * (th + 1))
                    for db in range(NDC):
                        gen = th * NDC + db
                        p = gen % 2
                        if gen >= 2:
                            te_e.wait_ge(ev, zz(ev) + gen - 1)
                        for hf in range(2):
                            te_e.wait_ge(*wlt[(gen, hf)])
                            for j in range(43):
                                ic = hf * 43 + j
                                i = None
                                for c2 in range(2):
                                    i = te_e.matmul(
                                        yps[p][:, c2 * 512:(c2 + 1) * 512],
                                        wdt[hf][:, j * 128:(j + 1) * 128],
                                        hres[:, ic * 1024 + c2 * 512:
                                             ic * 1024 + c2 * 512 + 512],
                                        start=(ic == 0), stop=(ic == NIB - 1))
                            i.then_inc(mmh, 1)
                            st.bump(mmh, 1)

            @blk.vector
            def _(ve):
                sc_y = 0.5 if pass_ in (0, 1) else 1.0
                for th in range(2):
                    for db in range(NDC):
                        gen = th * NDC + db
                        p = gen % 2
                        ve.wait_ge(mmh, zz(mmh) + 2 * gen + 2)
                        ve.wait_ge(xl, zz(xl) + 16 * (gen + 1))
                        if gen >= 1:
                            ve.wait_ge(xs, zz(xs) + 16 * gen)
                        for c2 in range(2):
                            i = ve.scalar_tensor_tensor(
                                od[0][:, c2 * 512:(c2 + 1) * 512],
                                yps[p][:, c2 * 512:(c2 + 1) * 512],
                                sc_y, xres[0][:, c2 * 512:(c2 + 1) * 512],
                                ALU.mult, ALU.add)
                        i.then_inc(ev, 1)
                        st.bump(ev, 1)

        assert st.val(mmh) == zz(mmh) + 2 * NGEN
        assert st.val(ev) == zz(ev) + NGEN
        assert st.val(xs) == zz(xs) + 16 * NGEN

    st.barrier_blk()


def phase_qkv(st, ctx, xn_sb):
    nc = st.nc
    qlb = [st.sem("qk_qlb0"), st.sem("qk_qlb1")]
    tbl = st.sem("qk_tbl")
    qm = st.sem("qk_qm")
    qe = st.sem("qk_qe")
    qswb = [st.sem("qk_swb0"), st.sem("qk_swb1")]
    qr = st.sem("qk_qr")
    qstb = [st.sem("qk_qst0"), st.sem("qk_qst1")]
    kstb = [st.sem("qk_kst0"), st.sem("qk_kst1")]
    vstb = [st.sem("qk_vst0"), st.sem("qk_vst1")]
    pl = st.sem("qk_pl")
    qlt, qswt = {}, {}
    z = {s.name: st.val(s) for s in (qm, qe, qr)}
    z.update({s.name: st.val(s) for s in qstb + kstb + vstb})

    def zz(s):
        return z[s.name]

    wsrc = [st.H["wq"], st.H["wk"]]

    with ExitStack() as pctx:
        e = pctx.enter_context
        nonce_sb = e(nc.sbuf_tensor("qk_nonce", [1, 1], U32))
        flag_sb = e(nc.sbuf_tensor("qk_flag", [1, 1], U32))

        with ExitStack() as qctx:
            e2 = qctx.enter_context
            c128 = e2(nc.sbuf_tensor("qk_c128", [128, T], F32))
            s128 = e2(nc.sbuf_tensor("qk_s128", [128, T], F32))
            wqt = [e2(nc.sbuf_tensor(f"qk_wqt{i}", [128, D], BF16)) for i in range(2)]
            X = [e2(nc.sbuf_tensor(f"qk_X{i}", [128, T], F32)) for i in range(2)]
            Xs = [e2(nc.sbuf_tensor(f"qk_Xs{i}", [128, T], F32)) for i in range(2)]
            ro = [e2(nc.sbuf_tensor(f"qk_ro{i}", [128, T], BF16)) for i in range(2)]
            qp = [e2(nc.psum_tensor(f"qk_qp{i}", [128, T], F32)) for i in range(2)]

            NH2 = 2 * H

            with nc.Block() as blk:
                @blk.gpsimd
                def _(g):
                    pid = g.partition_id()
                    myslot = nc.s_assert_within(pid % 2, min_val=0, max_val=1)
                    g.dma_start(out=c128[:, :], in_=st.H["c128"].ap()).then_inc(tbl, 16)
                    st.bump(tbl, 16)
                    g.dma_start(out=s128[:, :], in_=st.H["s128"].ap()).then_inc(tbl, 16)
                    st.bump(tbl, 16)
                    g.dma_start(out=nonce_sb[:, :], in_=st.H["nonce"].ap()).then_inc(tbl, 16)
                    st.bump(tbl, 16)
                    for hs in range(NH2):
                        wsel, hb = divmod(hs, H)
                        p = hs % 2
                        if hs >= 2:
                            g.wait_ge(qm, zz(qm) + hs - 1)
                        g.dma_start(
                            out=wqt[p][:, :].rearrange("p (a c) -> p a c", a=NDC),
                            in_=wsrc[wsel][:, hb * 128:(hb + 1) * 128]
                            .rearrange("(a p) c -> p a c", p=128)
                        ).then_inc(qlb[p], 16)
                        qlt[hs] = (qlb[p], st.bump(qlb[p], 16))
                        g.wait_ge(qe, zz(qe) + hs + 1)
                        if hs >= 2:
                            g.wait_ge(qr, zz(qr) + 2 * (hs - 1))
                        g.dma_start(out=Xs[p][0:64, :], in_=X[p][64:128, :]
                                    ).then_inc(qswb[p], 16)
                        st.bump(qswb[p], 16)
                        g.dma_start(out=Xs[p][64:128, :], in_=X[p][0:64, :]
                                    ).then_inc(qswb[p], 16)
                        qswt[hs] = (qswb[p], st.bump(qswb[p], 16))
                        if wsel == 1:
                            g.wait_ge(qr, zz(qr) + 2 * (hs + 1))
                            g.dma_start(
                                out=st.H["k_loc"][hb * 128:(hb + 1) * 128, :],
                                in_=ro[p][:, :]).then_inc(kstb[p], 16)
                            st.bump(kstb[p], 16)

                @blk.tensor
                def _(te_e):
                    for hs in range(NH2):
                        p = hs % 2
                        te_e.wait_ge(*qlt[hs])
                        if hs >= 2:
                            te_e.wait_ge(qe, zz(qe) + hs - 1)
                        for tc in range(NTC):
                            i = None
                            for hc in range(NDC):
                                i = te_e.matmul(
                                    qp[p][:, tc * 512:(tc + 1) * 512],
                                    wqt[p][:, hc * 128:(hc + 1) * 128],
                                    xn_sb[:, hc * T + tc * 512: hc * T + (tc + 1) * 512],
                                    start=(hc == 0), stop=(hc == NDC - 1))
                        i.then_inc(qm, 1)
                        st.bump(qm, 1)

                @blk.scalar
                def _(sa):
                    for hs in range(NH2):
                        p = hs % 2
                        sa.wait_ge(qm, zz(qm) + hs + 1)
                        if hs >= 2:
                            sa.wait_ge(qr, zz(qr) + 2 * (hs - 1))
                            sa.wait_ge(*qswt[hs - 2])
                        sa.copy(X[p][:, :], qp[p][:, :]).then_inc(qe, 1)
                        st.bump(qe, 1)

                tblv = st.val(tbl)

                @blk.vector
                def _(ve):
                    ve.wait_ge(tbl, tblv)
                    for hs in range(NH2):
                        p = hs % 2
                        ve.wait_ge(*qswt[hs])
                        if hs >= 2:
                            hp = hs - 2
                            if hp < H:
                                ve.wait_ge(qstb[p], zz(qstb[p]) + 16 * (hp // 2 + 1))
                            else:
                                ve.wait_ge(kstb[p], zz(kstb[p]) + 16 * ((hp - H) // 2 + 1))
                        ve.tensor_tensor(X[p][:, :], X[p][:, :], c128[:, :],
                                         ALU.mult)
                        ve.tensor_tensor(Xs[p][:, :], Xs[p][:, :], s128[:, :],
                                         ALU.mult).then_inc(qr, 1)
                        st.bump(qr, 1)
                        ve.wait_ge(qr, st.val(qr))
                        ve.tensor_tensor(ro[p][:, :], X[p][:, :], Xs[p][:, :],
                                         ALU.add).then_inc(qr, 1)
                        st.bump(qr, 1)

                @blk.sync
                def _(sy):
                    for hs in range(H):
                        hb = hs
                        p = hs % 2
                        sy.wait_ge(qr, zz(qr) + 2 * (hs + 1))
                        sy.dma_start(out=st.H["q"][hb * 128:(hb + 1) * 128, :],
                                     in_=ro[p][:, :]).then_inc(qstb[p], 16)
                        st.bump(qstb[p], 16)

            with nc.Block() as blkz:
                @blkz.sync
                def _(sy):
                    for s_ in qstb:
                        sy.wait_ge(s_, st.val(s_))

                @blkz.gpsimd
                def _(g):
                    for s_ in kstb + qswb + qlb + [tbl]:
                        g.wait_ge(s_, st.val(s_))
            st.barrier_blk()

            assert st.val(qm) == zz(qm) + NH2
            assert st.val(qe) == zz(qe) + NH2
            assert st.val(qr) == zz(qr) + 2 * NH2
            assert st.val(qstb[0]) + st.val(qstb[1]) == zz(qstb[0]) + zz(qstb[1]) + 16 * H
            assert st.val(kstb[0]) + st.val(kstb[1]) == zz(kstb[0]) + zz(kstb[1]) + 16 * H

        z2 = {s.name: st.val(s) for s in (qm, qe)}
        z2.update({s.name: st.val(s) for s in vstb})

        def zz2(s):
            return z2[s.name]

        with ExitStack() as vctx:
            e2 = vctx.enter_context
            wvt = [e2(nc.sbuf_tensor(f"qk_wvt{i}", [128, NDC * 512], BF16))
                   for i in range(2)]
            vsb = [e2(nc.sbuf_tensor(f"qk_vsb{i}", [128, 512], BF16)) for i in range(2)]
            vp = [e2(nc.psum_tensor(f"qk_vp{i}", [128, 512], F32)) for i in range(2)]
            vlt = {}
            NTB = 16

            with nc.Block() as blk:
                @blk.gpsimd
                def _(g):
                    pid = g.partition_id()
                    myslot = nc.s_assert_within(pid % 2, min_val=0, max_val=1)
                    n = 0
                    for dv in range(8):
                        p = dv % 2
                        if dv >= 2:
                            g.wait_ge(qm, zz2(qm) + NTB * (dv - 1))
                        g.dma_start(
                            out=wvt[p][:, :].rearrange("p (a c) -> p a c", a=NDC),
                            in_=st.H["wv"][:, dv * 512:(dv + 1) * 512]
                            .rearrange("(a p) c -> p a c", p=128)
                        ).then_inc(qlb[p], 16)
                        vlt[dv] = (qlb[p], st.bump(qlb[p], 16))
                        for tb in range(NTB):
                            pb = n % 2
                            g.wait_ge(qe, zz2(qe) + n + 1)
                            dst = st.H["v_loc"][tb * 128:(tb + 1) * 128,
                                                dv * 512:(dv + 1) * 512]
                            g.dma_start(out=dst, in_=vsb[pb][:, :]).then_inc(vstb[pb], 16)
                            st.bump(vstb[pb], 16)
                            n += 1

                @blk.tensor
                def _(te_e):
                    n = 0
                    for dv in range(8):
                        p = dv % 2
                        te_e.wait_ge(*vlt[dv])
                        for tb in range(NTB):
                            pb = n % 2
                            if n >= 2:
                                te_e.wait_ge(qe, zz2(qe) + n - 1)
                            i = None
                            for hc in range(NDC):
                                i = te_e.matmul(
                                    vp[pb][:, :],
                                    xn_sb[:, hc * T + tb * 128: hc * T + (tb + 1) * 128],
                                    wvt[p][:, hc * 512:(hc + 1) * 512],
                                    start=(hc == 0), stop=(hc == NDC - 1))
                            i.then_inc(qm, 1)
                            st.bump(qm, 1)
                            n += 1

                @blk.scalar
                def _(sa):
                    n = 0
                    for dv in range(8):
                        for tb in range(NTB):
                            pb = n % 2
                            sa.wait_ge(qm, zz2(qm) + n + 1)
                            if n >= 2:
                                sa.wait_ge(vstb[pb], zz2(vstb[pb]) + 16 * ((n - 2) // 2 + 1))
                            sa.copy(vsb[pb][:, :], vp[pb][:, :]).then_inc(qe, 1)
                            st.bump(qe, 1)
                            n += 1

            assert st.val(qm) == zz2(qm) + 128
            assert st.val(qe) == zz2(qe) + 128
            assert st.val(vstb[0]) + st.val(vstb[1]) == zz2(vstb[0]) + zz2(vstb[1]) + 16 * 128

        with nc.Block() as blk:
            @blk.gpsimd
            def _(g):
                pid = g.partition_id()
                myslot = nc.s_assert_within(pid % 2, min_val=0, max_val=1)
                pslot = nc.s_assert_within(1 - myslot, min_val=0, max_val=1)
                for s_ in qstb + kstb + vstb + qlb:
                    g.wait_ge(s_, st.val(s_))
                g.dma_start(out=st.H["k_sh"][bass.ds(myslot, 1), :, :].squeeze(0),
                            in_=st.H["k_loc"].ap()).then_inc(pl, 16)
                g.dma_start(out=st.H["v_sh"][bass.ds(myslot, 1), :, :].squeeze(0),
                            in_=st.H["v_loc"].ap()).then_inc(pl, 16)
                g.wait_ge(pl, 32)
                g.dma_start(out=st.H["flg"][bass.ds(myslot, 1), :].squeeze(0),
                            in_=nonce_sb[:1, :1]).then_inc(pl, 16)
                g.wait_ge(pl, 48)
                with (g.register("fl") as fl, g.register("nn") as nn,
                      g.register("ccr") as ccr, g.register("wt") as wt):
                    g.reg_load(nn, nonce_sb[:1, :1])
                    g.reg_mov(fl, 0)
                    g.reg_mov(wt, 48)
                    g.reg_alu(ccr, fl, nn, ALU.not_equal)
                    with g.While(ccr):
                        g.dma_start(out=flag_sb[:, :],
                                    in_=st.H["flg"][bass.ds(pslot, 1), :].unsqueeze(0)
                                    ).then_inc(pl, 16)
                        g.reg_add(wt, wt, 16)
                        g.wait_ge(pl, wt)
                        g.reg_load(fl, flag_sb[:1, :1])
                        g.reg_alu(ccr, fl, nn, ALU.not_equal)

    st.barrier_blk()


def phase_attn(st, ctx, ao_sb):
    nc = st.nc
    akl = [st.sem("at_akl0"), st.sem("at_akl1")]
    avl = [st.sem("at_avl0"), st.sem("at_avl1")]
    aml = st.sem("at_aml")
    asm = st.sem("at_asm")
    aex = st.sem("at_aex")
    amk = st.sem("at_amk")
    aog = st.sem("at_aog")
    arc = st.sem("at_arc")
    aev = st.sem("at_aev")

    z = {s.name: st.val(s) for s in (asm, aex, amk, aog, arc, aev)}

    def zz(s):
        return z[s.name]

    groups = []
    for h in range(H):
        for G in range(4):
            L = 4 * G + 4
            strips = []
            for r in range(2):
                for l in range(L):
                    smin2 = max(l - 4 * G, 0)
                    masked = l >= 4 * G
                    strips.append((r, l, smin2, masked))
            groups.append((h, G, strips))
    if DEV_ATTN_GROUPS is not None:
        groups = groups[:DEV_ATTN_GROUPS]
    cumstrip = []
    cummask = []
    cs = cm = 0
    for (_h, _G, strips) in groups:
        cumstrip.append(cs)
        cummask.append(cm)
        cs += len(strips)
        cm += sum(1 for s_ in strips if s_[3])
    NSTR = cs
    NM = len(groups)

    with ExitStack() as pctx:
        e = pctx.enter_context
        q_sb = [e(nc.sbuf_tensor(f"at_q{i}", [128, 512], BF16)) for i in range(2)]
        k_sb = [e(nc.sbuf_tensor(f"at_k{i}", [128, 4096], BF16)) for i in range(2)]
        v_sb = [e(nc.sbuf_tensor(f"at_v{i}", [128, 4096], BF16)) for i in range(2)]
        pts = [e(nc.sbuf_tensor(f"at_pts{i}", [128, 512], BF16)) for i in range(2)]
        msk = e(nc.sbuf_tensor("at_msk", [128, 256], BF16))
        rinv = [e(nc.sbuf_tensor(f"at_rinv{i}", [128, 512], F32)) for i in range(2)]
        if "atdbg" in st.taps:
            st.dbg_sb = e(nc.sbuf_tensor("at_dbg", [128, 2560], F32))
        stp = [e(nc.psum_tensor(f"at_stp{i}", [128, 512], F32)) for i in range(2)]
        sums = [e(nc.psum_tensor(f"at_sums{i}", [128, 512], F32)) for i in range(2)]
        pvp = [e(nc.psum_tensor(f"at_pvp{i}", [128, 512], F32)) for i in range(2)]
        qkt, vlt = {}, {}

        with nc.Block() as blk:
            @blk.sync
            def _(sy):
                for m, (h, G, strips) in enumerate(groups):
                    p = m % 2
                    L = 4 * G + 4
                    if m >= 2:
                        sy.wait_ge(aog, zz(aog) + cumstrip[m - 1])
                    sy.dma_start(out=q_sb[p][:, :],
                                 in_=st.H["q"][h * 128:(h + 1) * 128,
                                               G * 512:(G + 1) * 512]
                                 ).then_inc(akl[p], 16)
                    st.bump(akl[p], 16)
                    for r in range(2):
                        sy.dma_start(
                            out=k_sb[p][:, r * L * 128:(r + 1) * L * 128],
                            in_=st.H["k_sh"][r, h * 128:(h + 1) * 128, 0:L * 128]
                        ).then_inc(akl[p], 16)
                        st.bump(akl[p], 16)
                    qkt[m] = (akl[p], st.val(akl[p]))

            @blk.gpsimd
            def _(g):
                g.dma_start(out=msk[:, :].rearrange("p (a c) -> p a c", a=2),
                            in_=st.H["masks"].ap().rearrange("a p c -> p a c")
                            ).then_inc(aml, 16)
                st.bump(aml, 16)
                nh = max(h_ for (h_, _g, _s) in groups) + 1
                for h in range(nh):
                    p = h % 2
                    if h >= 2:
                        g.wait_ge(aog, zz(aog) + cumstrip[4 * (h - 1) - 1]
                                  + len(groups[4 * (h - 1) - 1][2]))
                    for r in range(2):
                        g.dma_start(
                            out=v_sb[p][:, r * 2048:(r + 1) * 2048]
                            .rearrange("p (a c) -> p a c", a=16),
                            in_=st.H["v_sh"][r, :, h * 128:(h + 1) * 128]
                            .rearrange("(a p) c -> p a c", p=128)
                        ).then_inc(avl[p], 16)
                        st.bump(avl[p], 16)
                    vlt[h] = (avl[p], st.val(avl[p]))

            @blk.tensor
            def _(te_e):
                for m, (h, G, strips) in enumerate(groups):
                    p = m % 2
                    L = 4 * G + 4
                    te_e.wait_ge(*qkt[m])
                    te_e.wait_ge(*vlt[h])
                    if m >= 2:
                        te_e.wait_ge(arc, zz(arc) + m - 1)
                        te_e.wait_ge(aev, zz(aev) + m - 1)
                    npv = 0
                    for j, (r, l, smin2, masked) in enumerate(strips):
                        cj = cumstrip[m] + j
                        if cj >= 2:
                            te_e.wait_ge(aex, zz(aex) + cj - 1)
                        n = 512 - smin2 * 128
                        te_e.matmul(stp[cj % 2][:, smin2 * 128:512],
                                    k_sb[p][:, (r * L + l) * 128:(r * L + l + 1) * 128],
                                    q_sb[p][:, smin2 * 128:512],
                                    start=True, stop=True).then_inc(asm, 1)
                        st.bump(asm, 1)
                        if j >= 1:
                            npv = _attn_onespv(st, te_e, groups, m, j - 1,
                                               sums[p], pvp[p], v_sb[h % 2], pts,
                                               zz, cumstrip, npv, cummask)
                    npv = _attn_onespv(st, te_e, groups, m, len(strips) - 1,
                                       sums[p], pvp[p], v_sb[h % 2], pts,
                                       zz, cumstrip, npv, cummask)

            @blk.scalar
            def _(sa):
                for m, (h, G, strips) in enumerate(groups):
                    for j, (r, l, smin2, masked) in enumerate(strips):
                        cj = cumstrip[m] + j
                        sa.wait_ge(asm, zz(asm) + cj + 1)
                        if cj >= 2:
                            sa.wait_ge(aog, zz(aog) + cj - 1)
                        n = 512 - smin2 * 128
                        sa.activation(pts[cj % 2][:, smin2 * 128:512],
                                      stp[cj % 2][:, smin2 * 128:512],
                                      AF.Exp, bias=st.EXPB,
                                      scale=QK_SCALE).then_inc(aex, 1)
                        st.bump(aex, 1)

            @blk.vector
            def _(ve):
                ve.wait_ge(aml, st.val(aml))
                for m, (h, G, strips) in enumerate(groups):
                    p = m % 2
                    for j, (r, l, smin2, masked) in enumerate(strips):
                        cj = cumstrip[m] + j
                        if masked:
                            ve.wait_ge(aex, zz(aex) + cj + 1)
                            c0 = (l - 4 * G) * 128
                            ve.tensor_tensor(pts[cj % 2][:, c0:c0 + 128],
                                             pts[cj % 2][:, c0:c0 + 128],
                                             msk[:, r * 128:(r + 1) * 128],
                                             ALU.mult).then_inc(amk, 1)
                            st.bump(amk, 1)
                    ve.wait_ge(aog, zz(aog) + cumstrip[m] + len(strips))
                    ve.reciprocal(rinv[p][:, :], sums[p][:, :]).then_inc(arc, 1)
                    st.bump(arc, 1)
                    ve.wait_ge(arc, st.val(arc))
                    ve.tensor_tensor(
                        ao_sb[:, h * T + G * 512: h * T + (G + 1) * 512],
                        pvp[p][:, :], rinv[p][:, :], ALU.mult).then_inc(aev, 1)
                    st.bump(aev, 1)
                    if m == 0 and "atdbg" in st.taps:
                        dbg = st.dbg_sb
                        ve.tensor_copy(dbg[:, 0:512], sums[p][:, :])
                        ve.tensor_copy(dbg[:, 512:1024], pvp[p][:, :])
                        ve.tensor_copy(dbg[:, 1024:1536], rinv[p][:, :])
                        ve.tensor_copy(dbg[:, 1536:2048], pts[(cumstrip[0] + 7) % 2][:, :])
                        ve.tensor_copy(dbg[:, 2048:2560], q_sb[p][:, :]).then_inc(aev, 1)
                        st.bump(aev, 1)

        if "atdbg" in st.taps:
            dtp = st.sem("at_dtp")
            with nc.Block() as blk2:
                @blk2.gpsimd
                def _(g):
                    g.wait_ge(st.sems["at_aev"], st.val(st.sems["at_aev"]))
                    g.dma_start(out=st.taps["atdbg"].ap(),
                                in_=st.dbg_sb[:, :]).then_inc(dtp, 16)
                    g.wait_ge(dtp, st.val(dtp) + 16)
                    st.bump(dtp, 16)

        assert st.val(asm) == zz(asm) + NSTR
        assert st.val(aex) == zz(aex) + NSTR
        assert st.val(aog) == zz(aog) + NSTR
        assert st.val(arc) == zz(arc) + NM

    st.barrier_blk()


def _attn_onespv(st, te_e, groups, m, j, sums_ps, pv_ps, v_sb, pts, zz,
                 cumstrip, npv, cummask=None):
    ALU_ = ALU
    h, G, strips = groups[m]
    r, l, smin2, masked = strips[j]
    L = 4 * G + 4
    cj = cumstrip[m] + j
    aex = st.sems["at_aex"]
    amk = st.sems["at_amk"]
    aog = st.sems["at_aog"]
    te_e.wait_ge(aex, zz(aex) + cj + 1)
    if masked:
        nm = cummask[m] + sum(1 for t_ in strips[:j + 1] if t_[3])
        te_e.wait_ge(amk, zz(amk) + nm)
    n = 512 - smin2 * 128
    first = (j == 0)
    te_e.matmul(sums_ps[:, smin2 * 128:512], st.ones128[:, :],
                pts[cj % 2][:, smin2 * 128:512],
                start=first, stop=(j == len(strips) - 1))
    lastj = (j == len(strips) - 1)
    for si, s in enumerate(range(max(l, 4 * G), 4 * G + 4)):
        c0 = (s - 4 * G) * 128
        i = te_e.matmul(pv_ps[:, c0:c0 + 128],
                        v_sb[:, (r * 16 + l) * 128:(r * 16 + l + 1) * 128],
                        pts[cj % 2][:, c0:c0 + 128],
                        start=(first and si == 0), stop=lastj)
        npv += 1
    i.then_inc(aog, 1)
    st.bump(aog, 1)
    return npv


def phase_oproj(st, ctx, ao_sb):
    nc = st.nc
    olb = [st.sem("op_olb0"), st.sem("op_olb1")]
    xlb = [st.sem("op_xlb0"), st.sem("op_xlb1")]
    om = st.sem("op_om")
    oe = st.sem("op_oe")
    ost = [st.sem("op_ost0"), st.sem("op_ost1")]
    olt, xlt, ostt = {}, {}, {}
    z = {s.name: st.val(s) for s in (om, oe)}
    zo = [st.val(ost[0]), st.val(ost[1])]

    def zz(s):
        return z[s.name]

    with ExitStack() as pctx:
        e = pctx.enter_context
        wot = [e(nc.sbuf_tensor(f"op_wot{i}", [128, D], BF16)) for i in range(2)]
        x1t = [e(nc.sbuf_tensor(f"op_x1t{i}", [128, T], F32)) for i in range(2)]
        x2t = [e(nc.sbuf_tensor(f"op_x2t{i}", [128, T], F32)) for i in range(2)]
        ops = [e(nc.psum_tensor(f"op_ps{i}", [128, T], F32)) for i in range(2)]

        with nc.Block() as blk:
            @blk.gpsimd
            def _(g):
                for db in range(NDC):
                    p = db % 2
                    if db >= 2:
                        g.wait_ge(om, zz(om) + 4 * (db - 1))
                    g.dma_start(
                        out=wot[p][:, :].rearrange("p (a c) -> p a c", a=NDC),
                        in_=st.H["wo"][:, db * 128:(db + 1) * 128]
                        .rearrange("(a p) c -> p a c", p=128)
                    ).then_inc(olb[p], 16)
                    olt[db] = (olb[p], st.bump(olb[p], 16))

            @blk.sync
            def _(sy):
                def store(dbs):
                    ps = dbs % 2
                    sy.wait_ge(oe, zz(oe) + 4 * (dbs + 1))
                    sy.dma_start(out=st.H["x2"][dbs * 128:(dbs + 1) * 128, :],
                                 in_=x2t[ps][:, :]).then_inc(ost[ps], 16)
                    ostt[dbs] = (ost[ps], st.bump(ost[ps], 16))

                for db in range(NDC):
                    p = db % 2
                    if db >= 2:
                        sy.wait_ge(oe, zz(oe) + 4 * (db - 1))
                    sy.dma_start(out=x1t[p][:, :],
                                 in_=st.H["x1"][db * 128:(db + 1) * 128, :]
                                 ).then_inc(xlb[p], 16)
                    xlt[db] = (xlb[p], st.bump(xlb[p], 16))
                    if db >= 1:
                        store(db - 1)
                store(NDC - 1)
                for s_ in ost:
                    sy.wait_ge(s_, st.val(s_))

            @blk.tensor
            def _(te_e):
                for db in range(NDC):
                    p = db % 2
                    te_e.wait_ge(*olt[db])
                    if db >= 2:
                        te_e.wait_ge(oe, zz(oe) + 4 * (db - 1))
                    for tc in range(NTC):
                        i = None
                        for dv in range(NDC):
                            i = te_e.matmul(
                                ops[p][:, tc * 512:(tc + 1) * 512],
                                wot[p][:, dv * 128:(dv + 1) * 128],
                                ao_sb[:, dv * T + tc * 512: dv * T + (tc + 1) * 512],
                                start=(dv == 0), stop=(dv == NDC - 1))
                        i.then_inc(om, 1)
                        st.bump(om, 1)

            @blk.vector
            def _(ve):
                for db in range(NDC):
                    p = db % 2
                    ve.wait_ge(*xlt[db])
                    for tc in range(NTC):
                        ve.wait_ge(om, zz(om) + 4 * db + tc + 1)
                        if db >= 2:
                            ve.wait_ge(ost[p], zo[p] + 16 * (db // 2))
                        ve.tensor_tensor(
                            x2t[p][:, tc * 512:(tc + 1) * 512],
                            ops[p][:, tc * 512:(tc + 1) * 512],
                            x1t[p][:, tc * 512:(tc + 1) * 512],
                            ALU.add).then_inc(oe, 1)
                        st.bump(oe, 1)


        assert st.val(om) == zz(om) + 4 * NDC
        assert st.val(oe) == zz(oe) + 4 * NDC

    st.barrier_blk()


def _rope_tables():
    inv = 1.0 / 10000.0 ** (np.arange(0, HD, 2)[: HD // 2].astype(np.float32) / HD)
    t = np.arange(S)
    f = np.outer(t, inv).astype(np.float32)
    return np.cos(f), np.sin(f)


def _stage(inputs):
    x = np.asarray(inputs["L0_attn_x"])
    wq = np.asarray(inputs["L0_attn_wq"])
    wk = np.asarray(inputs["L0_attn_wk"])
    wv = np.asarray(inputs["L0_attn_wv"])
    wo = np.asarray(inputs["L0_attn_wo"])

    perm1 = np.concatenate([np.arange(0, HD, 2), np.arange(1, HD, 2)])
    perm = np.concatenate([h * HD + perm1 for h in range(H)])
    wq_p = np.ascontiguousarray(wq[:, perm])
    wk_p = np.ascontiguousarray(wk[:, perm])

    cosf, sinf = _rope_tables()

    shared = dict(
        wq=wq_p, wk=wk_p, wv=np.ascontiguousarray(wv), wo=np.ascontiguousarray(wo),
        pg=np.ascontiguousarray(np.asarray(inputs["L0_ffn_gate_Wq"], np.int32)),
        pu=np.ascontiguousarray(np.asarray(inputs["L0_ffn_up_Wq"], np.int32)),
        pd=np.ascontiguousarray(np.asarray(inputs["L0_ffn_down_Wq"], np.int32)),
        sg=np.ascontiguousarray(np.asarray(inputs["L0_ffn_gate_scale"], np.float32)),
        zg=np.ascontiguousarray(np.asarray(inputs["L0_ffn_gate_zero"], np.float32)),
        su=np.ascontiguousarray(np.asarray(inputs["L0_ffn_up_scale"], np.float32)),
        zu=np.ascontiguousarray(np.asarray(inputs["L0_ffn_up_zero"], np.float32)),
        sd=np.ascontiguousarray(np.asarray(inputs["L0_ffn_down_scale"], np.float32)),
        zd=np.ascontiguousarray(np.asarray(inputs["L0_ffn_down_zero"], np.float32)),
    )

    rng = np.random.default_rng()
    nonce = np.uint32(rng.integers(1, 2**31))

    tri = np.tril(np.ones((128, 128), np.float32)).T
    ones = np.ones((128, 128), np.float32)
    zeros = np.zeros((128, 128), np.float32)

    per_core = []
    for c in range(8):
        b, par = divmod(c, 2)
        cols = np.concatenate([np.arange(j * 128, (j + 1) * 128)
                               for j in range(par, 32, 2)])
        xs = np.ascontiguousarray(x[b].T[:, cols], dtype=np.float32)
        ct = cosf[cols].T
        stt = sinf[cols].T
        c128 = np.ascontiguousarray(np.concatenate([ct, ct], axis=0), np.float32)
        s128 = np.ascontiguousarray(np.concatenate([-stt, stt], axis=0), np.float32)
        if par == 0:
            masks = np.stack([tri, zeros])
        else:
            masks = np.stack([ones, tri])
        per_core.append(dict(
            x=xs, c128=c128, s128=s128,
            masks=masks.astype(ml_dtypes.bfloat16),
            nonce=np.array([[nonce]], np.uint32),
        ))
    return shared, per_core, cols


def build_program():
    nc = bass.Bass("TRN2", target_bir_lowering=False)

    x_h = nc.dram_tensor("x", [D, T], F32, kind="ExternalInput")
    wq_h = nc.dram_tensor("wq", [D, D], F32, kind="ExternalInput")
    wk_h = nc.dram_tensor("wk", [D, D], F32, kind="ExternalInput")
    wv_h = nc.dram_tensor("wv", [D, D], F32, kind="ExternalInput")
    wo_h = nc.dram_tensor("wo", [D, D], F32, kind="ExternalInput")
    pg_h = nc.dram_tensor("pg", [ID, HP], I32, kind="ExternalInput")
    pu_h = nc.dram_tensor("pu", [ID, HP], I32, kind="ExternalInput")
    pd_h = nc.dram_tensor("pd", [D, IP], I32, kind="ExternalInput")
    sg_h = nc.dram_tensor("sg", [ID, GRP], F32, kind="ExternalInput")
    zg_h = nc.dram_tensor("zg", [ID, GRP], F32, kind="ExternalInput")
    su_h = nc.dram_tensor("su", [ID, GRP], F32, kind="ExternalInput")
    zu_h = nc.dram_tensor("zu", [ID, GRP], F32, kind="ExternalInput")
    sd_h = nc.dram_tensor("sd", [D, IP // 32], F32, kind="ExternalInput")
    zd_h = nc.dram_tensor("zd", [D, IP // 32], F32, kind="ExternalInput")
    c128_h = nc.dram_tensor("c128", [128, T], F32, kind="ExternalInput")
    s128_h = nc.dram_tensor("s128", [128, T], F32, kind="ExternalInput")
    masks_h = nc.dram_tensor("masks", [2, 128, 128], BF16, kind="ExternalInput")
    nonce_h = nc.dram_tensor("nonce", [1, 1], U32, kind="ExternalInput")

    out_h = nc.dram_tensor("out", [D, T], F32, kind="ExternalOutput")

    x1_h = nc.dram_tensor("x1", [D, T], F32, kind="Internal")
    x2_h = nc.dram_tensor("x2", [D, T], F32, kind="Internal")
    hbuf_h = nc.dram_tensor("hbuf", [ID, T], BF16, kind="Internal")
    q_hb = nc.dram_tensor("qbuf", [D, T], BF16, kind="Internal")
    wgT_h = nc.dram_tensor("wgT", [D, ID], BF16, kind="Internal")
    wuT_h = nc.dram_tensor("wuT", [D, ID], BF16, kind="Internal")
    wdT_h = nc.dram_tensor("wdT", [ID, D], BF16, kind="Internal")
    k_loc = nc.dram_tensor("k_loc", [D, T], BF16, kind="Internal")
    v_loc = nc.dram_tensor("v_loc", [T, D], BF16, kind="Internal")
    k_sh = nc.dram_tensor("k_sh", [2, D, T], BF16, kind="Internal", addr_space="Shared")
    v_sh = nc.dram_tensor("v_sh", [2, T, D], BF16, kind="Internal", addr_space="Shared")
    flg_sh = nc.dram_tensor("flg_sh", [2, 1], U32, kind="Internal", addr_space="Shared")

    st = St(nc)
    st.H = dict(x=x_h, wq=wq_h, wk=wk_h, wv=wv_h, wo=wo_h,
                pg=pg_h, pu=pu_h, pd=pd_h, sg=sg_h, zg=zg_h, su=su_h, zu=zu_h,
                sd=sd_h, zd=zd_h, c128=c128_h, s128=s128_h, masks=masks_h,
                nonce=nonce_h, out=out_h, x1=x1_h, x2=x2_h, hbuf=hbuf_h,
                q=q_hb, wgT=wgT_h, wuT=wuT_h, wdT=wdT_h,
                k_loc=k_loc, v_loc=v_loc,
                k_sh=k_sh, v_sh=v_sh, flg=flg_sh)

    for tname, shape, dt in [("xn", [D, T], BF16), ("h", [ID, T], BF16),
                             ("wdTt", [ID, D], BF16),
                             ("x1t", [D, T], F32), ("qt", [D, T], BF16),
                             ("kt", [2, D, T], BF16), ("vt", [2, T, D], BF16),
                             ("aot", [D, T], BF16), ("x2t", [D, T], F32),
                             ("atdbg", [128, 2560], F32)]:
        if _taps_want(tname):
            st.taps[tname] = nc.dram_tensor("tap_" + tname, shape, dt,
                                            kind="ExternalOutput")

    def barrier_blk():
        with nc.Block() as blk:
            st.barrier(blk)
    st.barrier_blk = barrier_blk

    build_body(st)
    return nc, st


def _tap_out(st, name, sb, rows=None):
    if name not in st.taps:
        return
    nc = st.nc
    hdl = st.taps[name]
    nch = hdl.shape[0] // 128
    tp = st.sem("tap")
    with nc.Block() as blk:
        @blk.gpsimd
        def _(g):
            for ch in range(nch):
                g.dma_start(out=hdl[ch * 128:(ch + 1) * 128, :],
                            in_=sb[:, ch * T:(ch + 1) * T]).then_inc(tp, 16)
                st.bump(tp, 16)
            g.wait_ge(tp, st.val(tp))
    st.barrier_blk()


def build_body(st):
    nc = st.nc
    with ExitStack() as ctx:
        phase_init(st, ctx)

        with ExitStack() as span1:
            xn_sb = span1.enter_context(nc.sbuf_tensor("xn1s", [128, NDC * T], BF16))
            phase_ln(st, ctx, st.H["x"], xn_sb, "ln1")
            _tap_out(st, "xn", xn_sb)
            if PHASE_STOP == "ln1":
                finalize_stub(st)
                return
            phase_ffn_a(st, ctx, xn_sb, 0)
        if PHASE_STOP == "ffn1a":
            _tap_h(st)
            finalize_stub(st)
            return

        phase_wd(st, ctx)
        if PHASE_STOP == "wd":
            _tap_hbm(st, "wdTt", st.H["wdT"])
            finalize_stub(st)
            return
        phase_ffn_b(st, ctx, 0, st.H["x"], st.H["x1"])
        if PHASE_STOP == "ffn1":
            _tap_hbm(st, "x1t", st.H["x1"])
            finalize_stub(st)
            return

        with ExitStack() as span2:
            xn_sb = span2.enter_context(nc.sbuf_tensor("xn2s", [128, NDC * T], BF16))
            phase_ln(st, ctx, st.H["x1"], xn_sb, "ln2")
            phase_qkv(st, ctx, xn_sb)
        if PHASE_STOP == "qkv":
            _tap_hbm(st, "qt", st.H["q"])
            _tap_hbm(st, "kt", st.H["k_sh"])
            _tap_hbm(st, "vt", st.H["v_sh"])
            finalize_stub(st)
            return

        with ExitStack() as span3:
            ao_sb = span3.enter_context(nc.sbuf_tensor("ao_s", [128, NDC * T], BF16))
            phase_attn(st, ctx, ao_sb)
            if PHASE_STOP == "attn":
                _tap_out(st, "aot", ao_sb)
                finalize_stub(st)
                return
            phase_oproj(st, ctx, ao_sb)
        if PHASE_STOP == "oproj":
            _tap_hbm(st, "x2t", st.H["x2"])
            finalize_stub(st)
            return

        with ExitStack() as span4:
            xn_sb = span4.enter_context(nc.sbuf_tensor("xn3s", [128, NDC * T], BF16))
            phase_ln(st, ctx, st.H["x2"], xn_sb, "ln3")
            phase_ffn_a(st, ctx, xn_sb, 1)
        phase_ffn_b(st, ctx, 1, st.H["x2"], st.H["out"])


def _tap_hbm(st, name, src_h):
    if name not in st.taps:
        return
    nc = st.nc
    tp = st.sem("tap")
    with nc.Block() as blk:
        @blk.gpsimd
        def _(g):
            g.dma_start(out=st.taps[name].ap(), in_=src_h.ap()).then_inc(tp, 16)
            st.bump(tp, 16)
            g.wait_ge(tp, st.val(tp))
    st.barrier_blk()


def _tap_h(st):
    if "h" not in st.taps:
        return
    nc = st.nc
    tp = st.sem("tap")
    with nc.Block() as blk:
        @blk.gpsimd
        def _(g):
            g.dma_start(out=st.taps["h"].ap(), in_=st.H["hbuf"].ap()).then_inc(tp, 16)
            st.bump(tp, 16)
            g.wait_ge(tp, st.val(tp))
    st.barrier_blk()


def finalize_stub(st):
    nc = st.nc
    fz = st.sem("fin")
    with nc.Block() as blk:
        @blk.sync
        def _(sy):
            for dc in range(NDC):
                sy.dma_start(out=st.H["out"][dc * 128:(dc + 1) * 128, :],
                             in_=st.H["x"][dc * 128:(dc + 1) * 128, :]).then_inc(fz, 16)
                st.bump(fz, 16)
            sy.wait_ge(fz, st.val(fz))


def finalize_waits(st, outsem):
    nc = st.nc
    with nc.Block() as blk:
        @blk.sync
        def _(sy):
            sy.wait_ge(outsem, st.val(outsem))


_CACHE = {}


def kernel(**inputs):
    shared, per_core, _ = _stage(inputs)
    if "nc" not in _CACHE:
        _CACHE["nc"] = build_program()
    nc, st = _CACHE["nc"]
    in_maps = [dict(shared, **pc) for pc in per_core]
    res = run_bass_kernel_spmd(nc, in_maps, core_ids=list(range(8)))
    out = np.zeros((4, S, D), np.float32)
    for c in range(8):
        b, par = divmod(c, 2)
        o = res.results[c]["out"]
        cols = np.concatenate([np.arange(j * 128, (j + 1) * 128)
                               for j in range(par, 32, 2)])
        out[b][cols, :] = o.T
    if DEBUG_TAPS:
        kernel.last_taps = [{k: v for k, v in r.items() if k.startswith("tap_")}
                            for r in res.results]
    return out
